# version 14
# baseline (speedup 1.0000x reference)
import numpy as np
from contextlib import ExitStack

import concourse.bass as bass
import concourse.mybir as mybir
from concourse.bass_utils import run_bass_kernel_spmd
from concourse.bass_types import AP as _APC

F32 = mybir.dt.float32
BF16 = mybir.dt.bfloat16
AF = mybir.ActivationFunctionType
OP = mybir.AluOpType
AXL = mybir.AxisListType

B, S, D = 2, 4096, 4096
H, PD, G, NS = 128, 64, 8, 128
CHUNK = 128
DIN = H * PD
CONVD = DIN + 2 * G * NS
PROJ = DIN + CONVD + H
EPS = 1e-5
NCORES = 8
T = (B * S) // NCORES
NCH = T // CHUNK
KFF, RANK = 64, 16
FFSCALE = 2.0
HBC0 = DIN
DT0 = DIN + CONVD
NEG = -3.0e38
RG = [[0, 1, 2, 3], [4, 5, 6, 7]]


class Ctr:
    def __init__(self, handle):
        self.h = handle
        self.n = 0

    def tick(self, amt=1):
        self.n += amt
        return self.n



def _ap_range(ap):
    try:
        t = ap.tensor
        esz = mybir.dt.size(t.dtype)
        lo = ap.offset
        hi = ap.offset
        for step, count in list(ap.ap)[1:]:
            if count > 1:
                d = step * (count - 1)
                if d > 0:
                    hi += d
                else:
                    lo += d
        return (t.name, lo * esz, (hi + 1) * esz)
    except Exception:
        return None


_RW_METHS = {
    "tensor_copy", "tensor_tensor", "tensor_add", "tensor_sub", "tensor_mul",
    "tensor_max", "tensor_scalar", "tensor_single_scalar", "tensor_scalar_add",
    "tensor_scalar_sub", "tensor_scalar_mul", "tensor_scalar_max", "tensor_scalar_min",
    "tensor_relu", "tensor_reduce", "reduce_max", "reduce_sum",
    "scalar_tensor_tensor", "tensor_tensor_scan", "reciprocal", "bn_stats",
    "bn_aggr", "max", "match_replace", "select", "activation",
    "copy", "add", "mul", "memset", "memzero",
}


def _rw_sets(meth, a, k):
    W_, R_ = [], []
    aps = [x for x in a if hasattr(x, "tensor")]
    kaps = {kk: vv for kk, vv in k.items() if hasattr(vv, "tensor")}
    if meth in _RW_METHS:
        if aps:
            W_.append(aps[0])
            R_.extend(aps[1:])
        if "accum_out" in kaps:
            W_.append(kaps.pop("accum_out"))
        R_.extend(kaps.values())
    return W_, R_


def _overlap(r1, r2):
    return r1 is not None and r2 is not None and r1[0] == r2[0] and r1[1] < r2[2] and r2[1] < r1[2]


def _coalesce_waits(Q):
    for q in Q:
        old = Q[q]
        new = []
        run = {}
        order = []

        def flushrun():
            for h in order:
                new.append(("wait_ge", (h, run[h]), {}, None))
            run.clear()
            order.clear()

        for item in old:
            meth, a, k, inc = item
            if meth == "wait_ge":
                h, v = a
                if h not in run:
                    order.append(h)
                    run[h] = v
                else:
                    run[h] = max(run[h], v)
            else:
                flushrun()
                new.append(item)
        flushrun()
        Q[q] = new


def _insert_pipeline_edges(Q, pipe_sems):
    for q in ("v", "s", "g"):
        psem, pctr = pipe_sems[q]
        old = Q[q]
        new = []
        opsinfo = []
        by_tensor = {}
        sem_counts = {}
        inc_val_of_op = {}
        covered = -1
        for item in old:
            meth, a, k, inc = item
            if meth == "wait_ge":
                new.append(item)
                continue
            W_, R_ = _rw_sets(meth, a, k)
            wr = [r for r in (_ap_range(x) for x in W_) if r]
            rr = [r for r in (_ap_range(x) for x in R_) if r]
            conflict = -1
            for rng, iswrite in [(r, True) for r in wr] + [(r, False) for r in rr]:
                for (lo, hi, opno, w2) in by_tensor.get(rng[0], ()):
                    if opno <= covered or opno <= conflict:
                        continue
                    if not (iswrite or w2):
                        continue
                    if rng[1] < hi and lo < rng[2]:
                        conflict = max(conflict, opno)
            if conflict >= 0:
                nidx = opsinfo[conflict][0]
                m2, a2, k2, inc2 = new[nidx]
                if inc2 is None:
                    pctr[0] += 1
                    new[nidx] = (m2, a2, k2, (psem, 1))
                    inc_val_of_op[conflict] = (psem, pctr[0])
                ws, wv = inc_val_of_op[conflict]
                new.append(("wait_ge", (ws, wv), {}, None))
                covered = conflict
            new.append(item)
            opno = len(opsinfo)
            opsinfo.append((len(new) - 1, wr, rr))
            if inc is not None:
                s_, amt = inc
                cum = sem_counts.get(id(s_), 0) + amt
                sem_counts[id(s_)] = cum
                inc_val_of_op[opno] = (s_, cum)
            for rng in wr:
                by_tensor.setdefault(rng[0], []).append((rng[1], rng[2], opno, True))
            for rng in rr:
                by_tensor.setdefault(rng[0], []).append((rng[1], rng[2], opno, False))
            if opno % 512 == 0:
                for nm in list(by_tensor):
                    by_tensor[nm] = [e for e in by_tensor[nm] if e[2] > covered]
        Q[q] = new


def build_program(dbg=(), stop_after=None, want_queues=False, p1_limit=None):
    dbg = set(dbg)
    nc = bass.Bass("TRN2", target_bir_lowering=False)

    x_in = nc.dram_tensor("x", [T, D], F32, kind="ExternalInput").ap()
    xh_in = nc.dram_tensor("xhalo", [4, D], F32, kind="ExternalInput").ap()
    w1t = nc.dram_tensor("w1t", [D, PROJ], F32, kind="ExternalInput").ap()
    wot = nc.dram_tensor("wot", [DIN, D], F32, kind="ExternalInput").ap()
    at_d = nc.dram_tensor("atw", [D, KFF * RANK], F32, kind="ExternalInput").ap()
    bm_d = nc.dram_tensor("bmw", [KFF * RANK, D], F32, kind="ExternalInput").ap()
    wrt_d = nc.dram_tensor("wrt", [D, KFF], F32, kind="ExternalInput").ap()
    cw_d = nc.dram_tensor("convw", [CONVD, 4], F32, kind="ExternalInput").ap()
    cb_d = nc.dram_tensor("convb", [CONVD], F32, kind="ExternalInput").ap()
    dtb_d = nc.dram_tensor("dtb", [H], F32, kind="ExternalInput").ap()
    alog_d = nc.dram_tensor("alog", [H], F32, kind="ExternalInput").ap()
    dp_d = nc.dram_tensor("dprow", [1, H], F32, kind="ExternalInput").ap()
    nw_d = nc.dram_tensor("normw", [D], F32, kind="ExternalInput").ap()
    gw_d = nc.dram_tensor("gnormw", [DIN], F32, kind="ExternalInput").ap()
    id_d = nc.dram_tensor("ident", [128, 128], F32, kind="ExternalInput").ap()
    tri_d = nc.dram_tensor("triu01", [128, 128], F32, kind="ExternalInput").ap()
    sel_d = nc.dram_tensor("selmask", [128, 4], F32, kind="ExternalInput").ap()

    out_d = nc.dram_tensor("out", [T, D], F32, kind="ExternalOutput").ap()

    hidT_d = nc.dram_tensor("hidT_d", [NCH, CHUNK, DIN], BF16, kind="Internal").ap()
    bc_dd = nc.dram_tensor("bc_d", [16, 128, T], BF16, kind="Internal").ap()
    gate_dd = nc.dram_tensor("gate_d", [64, 128, T], BF16, kind="Internal").ap()
    ffn_dd = nc.dram_tensor("ffn_d", [T, D], BF16, kind="Internal").ap()
    ygt_dd = nc.dram_tensor("ygt_d", [64, NCH, 128, 128], BF16, kind="Internal").ap()
    flat_dd = nc.dram_tensor("flat_d", [NCH, H * CHUNK], F32, kind="Internal").ap()
    ccs_in = nc.dram_tensor("ccs_in", [H, NS * PD], BF16, kind="Internal").ap()
    ccs_out = nc.dram_tensor("ccs_out", [4 * H, NS * PD], BF16, kind="Internal").ap()
    ccl_in = nc.dram_tensor("ccl_in", [H, 1], F32, kind="Internal").ap()
    ccl_out = nc.dram_tensor("ccl_out", [4 * H, 1], F32, kind="Internal").ap()
    sin_dd = nc.dram_tensor("sin_d", [H, NS * PD], F32, kind="Internal").ap()

    dbg_outs = {}

    def dbg_t(name, shape, dt=F32):
        if name in dbg and name not in dbg_outs:
            dbg_outs[name] = nc.dram_tensor("dbg_" + name, shape, dt, kind="ExternalOutput").ap()
        return dbg_outs.get(name)

    Q = {k: [] for k in "tvsyg"}
    marks = {}
    mark_sems = {}

    def mk(q):
        def em(meth, *a, inc=None, **k):
            Q[q].append((meth, a, k, inc))
        return em
    emt, emv, ems, emy, emg = mk("t"), mk("v"), mk("s"), mk("y"), mk("g")
    EM = dict(t=emt, v=emv, s=ems, y=emy, g=emg)

    def inc_last(q, ctr, amt=1):
        meth, aa, kk, inc = Q[q][-1]
        assert inc is None and meth != "wait_ge", (meth, inc)
        t = ctr.tick(amt)
        Q[q][-1] = (meth, aa, kk, (ctr.h, amt))
        return t

    with ExitStack() as CX:
        E = CX.enter_context
        sPE = Ctr(E(nc.semaphore("sPE")))
        sV = Ctr(E(nc.semaphore("sV")))
        sS = Ctr(E(nc.semaphore("sS")))
        sCC = Ctr(E(nc.semaphore("sCC")))

        def W(q, ctr, val=None):
            v = ctr.n if val is None else val
            if v > 0:
                EM[q]("wait_ge", ctr.h, v)

        class DStream:

            def __init__(self, q, name):
                self.q = q
                self.ctr = Ctr(E(nc.semaphore(name)))
                self.open = False

            def dma(self, out, in_):
                if not self.open:
                    W(self.q, self.ctr)
                    self.open = True
                t = self.ctr.tick(16)
                EM[self.q]("dma_start", out, in_, inc=(self.ctr.h, 16))
                return t

            def close(self):
                self.open = False
                return self.ctr.n

            def one(self, out, in_):
                self.dma(out, in_)
                return self.close()

        stMISC = DStream("y", "stMISC")
        stX = DStream("y", "stX")
        stST = DStream("y", "stST")
        stGT = DStream("y", "stGT")
        stCH = DStream("y", "stCH")
        stGF = DStream("y", "stGF")
        stFR = DStream("y", "stFR")
        stSI = DStream("y", "stSI")
        stYG = DStream("y", "stYG")
        stFS = DStream("y", "stFS")
        stO = DStream("y", "stO")
        stX4 = DStream("y", "stX4")
        stW1 = DStream("g", "stW1")
        stW1b = DStream("g", "stW1b")
        stWB = DStream("g", "stWB")
        stWO = DStream("g", "stWO")
        stPK = DStream("g", "stPK")
        YSTREAMS = [stMISC, stX, stST, stGT, stCH, stGF, stFR, stSI, stYG, stFS, stO, stX4]
        GSTREAMS = [stW1, stW1b, stWB, stWO, stPK]

        def mark(name):
            marks[name] = {q: len(Q[q]) for q in Q}
            mark_sems[name] = ([(s.ctr.h, s.ctr.n) for s in YSTREAMS],
                               [(s.ctr.h, s.ctr.n) for s in GSTREAMS])

        def barrier():
            for q in "tvsyg":
                W(q, sPE)
                W(q, sV)
                W(q, sS)
                for st_ in YSTREAMS + GSTREAMS:
                    W(q, st_.ctr)

        pA = E(nc.psum_tensor("pA", [128, 512], F32))
        pB = E(nc.psum_tensor("pB", [128, 512], F32))
        pC = E(nc.psum_tensor("pC", [128, 512], F32))
        pD = E(nc.psum_tensor("pD", [128, 512], F32))
        pE_ = E(nc.psum_tensor("pE", [128, 512], F32))
        pF = E(nc.psum_tensor("pF", [128, 512], F32))
        pG = E(nc.psum_tensor("pG", [128, 1024], BF16))
        pH = E(nc.psum_tensor("pH", [128, 1024], BF16))

        identb = E(nc.sbuf_tensor("identb", [128, 128], BF16))
        identf = E(nc.sbuf_tensor("identf", [128, 128], F32))
        triu = E(nc.sbuf_tensor("triu", [128, 128], F32))
        selm = E(nc.sbuf_tensor("selm", [128, 4], F32))
        ones1 = E(nc.sbuf_tensor("ones1", [1, 128], F32))
        rowbuf = E(nc.sbuf_tensor("rowbuf", [1, 128], F32))
        nw_sb = E(nc.sbuf_tensor("nw_sb", [128, 32], F32))
        gw_sb = E(nc.sbuf_tensor("gw_sb", [128, 64], F32))
        cw_sb = E(nc.sbuf_tensor("cw_sb", [128, 80 * 4], F32))
        cb_sb = E(nc.sbuf_tensor("cb_sb", [128, 80], F32))
        dtb_sb = E(nc.sbuf_tensor("dtb_sb", [128, 1], F32))
        acol = E(nc.sbuf_tensor("acol", [128, 1], F32))
        dpb = E(nc.sbuf_tensor("dpb", [128, 128], F32))
        stats = E(nc.sbuf_tensor("stats", [128, 80], F32))
        smallA = E(nc.sbuf_tensor("smallA", [128, 32], F32))
        aclast = E(nc.sbuf_tensor("aclast", [128, NCH], F32))
        logdt = E(nc.sbuf_tensor("logdt", [128, 1], F32))
        sVp = Ctr(E(nc.semaphore("sVp")))
        sSp = Ctr(E(nc.semaphore("sSp")))
        sGp = Ctr(E(nc.semaphore("sGp")))
        sG = Ctr(E(nc.semaphore("sG")))

        stMISC.dma(identf[:], id_d[:, :])
        stMISC.dma(triu[:], tri_d[:, :])
        stMISC.dma(selm[:], sel_d[:, :])
        stMISC.dma(dtb_sb[:], dtb_d.rearrange("(p one) -> p one", one=1))
        stMISC.dma(smallA[:, 0:1], alog_d.rearrange("(p one) -> p one", one=1))
        stMISC.dma(rowbuf[:], dp_d[:, :])
        for i in range(32):
            stMISC.dma(nw_sb[:, i:i + 1], nw_d[i * 128:(i + 1) * 128].rearrange("(p one) -> p one", one=1))
        for i in range(64):
            stMISC.dma(gw_sb[:, i:i + 1], gw_d[i * 128:(i + 1) * 128].rearrange("(p one) -> p one", one=1))
        stMISC.dma(cw_sb[:].rearrange("p (b k) -> p b k", k=4), cw_d.rearrange("(b p) k -> p b k", p=128))
        for i in range(80):
            stMISC.dma(cb_sb[:, i:i + 1], cb_d[i * 128:(i + 1) * 128].rearrange("(p one) -> p one", one=1))
        pre_io = stMISC.close()

        W("v", stMISC.ctr, pre_io)
        emv("memset", stats[:], 0.0)
        emv("memset", smallA[:, 2:32], 0.0)
        emv("memset", ones1[:], 1.0)
        emv("tensor_copy", identb[:], identf[:])
        W("s", stMISC.ctr, pre_io)
        ems("activation", smallA[:, 1:2], smallA[:, 0:1], AF.Exp, inc=(sS.h, 1))
        tS0 = sS.tick()
        W("v", sS, tS0)
        emv("tensor_scalar_mul", acol[:], smallA[:, 1:2], -1.0)
        tOnes = inc_last("v", sV)
        W("t", sV, tOnes)
        W("t", stMISC.ctr, pre_io)
        emt("matmul", pE_[:, 0:128], ones1[0:1, :], rowbuf[0:1, :], start=True, stop=True,
            inc=(sPE.h, 1))
        tPE0 = sPE.tick()
        W("v", sPE, tPE0)
        emv("tensor_copy", dpb[:], pE_[:, 0:128], inc=(sV.h, 1))
        sV.tick()

        class TPool:
            def __init__(self):
                self.bank = 0
                self.items = []
                self.bank_free = [0, 0]
                self.last_pe = 0

            def add(self, in_ap, evac):
                self.items.append((in_ap, evac))
                if len(self.items) == 8:
                    self.flush()

            def flush(self):
                if not self.items:
                    return
                bankT = pG if self.bank == 0 else pH
                W("t", sV, self.bank_free[self.bank])
                n = len(self.items)
                for i, (in_ap, _) in enumerate(self.items):
                    kk = in_ap.shape[0]
                    emt("transpose", bankT[:, i * 128:i * 128 + kk], in_ap, identb[0:kk, 0:kk],
                        inc=(sPE.h, 1) if i == n - 1 else None)
                self.last_pe = sPE.tick()
                W("v", sPE, self.last_pe)
                for i, (_, evac) in enumerate(self.items):
                    evac(bankT[:, i * 128:i * 128 + 128])
                self.bank_free[self.bank] = inc_last("v", sV)
                self.bank = 1 - self.bank
                self.items = []

            def drain(self):
                self.flush()
                return max(self.bank_free)

        pool = TPool()

        with ExitStack() as AX_:
            EA = AX_.enter_context
            Rbuf = EA(nc.sbuf_tensor("Rbuf", [128, H * PD], F32))
            btok = EA(nc.sbuf_tensor("btok", [128, 64 * 128], BF16))
            dt_sb = EA(nc.sbuf_tensor("dt_sb", [128, T], F32))
            dtT = EA(nc.sbuf_tensor("dtT", [128, T], F32))
            acs = EA(nc.sbuf_tensor("acs", [128, T], F32))
            ddt = EA(nc.sbuf_tensor("ddt", [128, T], F32))
            dtotb = EA(nc.sbuf_tensor("dtotb", [128, T], F32))

            emv("memset", Rbuf[:], 0.0, inc=(sV.h, 1))
            sV.tick()

            with ExitStack() as PX:
                EP = PX.enter_context
                hT = EP(nc.sbuf_tensor("hT", [128, 32 * 1027], BF16))
                w1buf = EP(nc.sbuf_tensor("w1buf", [128, 2 * 4096], BF16))
                scr = EP(nc.sbuf_tensor("scr", [128, 3 * 1024], F32))
                hbcraw = EP(nc.sbuf_tensor("hbcraw", [128, 1027], F32))
                hfeat = EP(nc.sbuf_tensor("hfeat", [128, 1024], BF16))
                hstage = EP(nc.sbuf_tensor("hstage", [128, 2 * 1024], BF16))
                bct = EP(nc.sbuf_tensor("bct", [128, 1024], BF16))
                xh8 = EP(nc.sbuf_tensor("xh8", [128, 8 * 128], BF16))
                bdb = EP(nc.sbuf_tensor("bdb", [128, 2 * 16 * 128], BF16))
                gatef = EP(nc.sbuf_tensor("gatef", [128, 2 * 1024], BF16))
                bnst = EP(nc.sbuf_tensor("bnst", [128, 48], F32))

                def hcol(dt_i, tok):
                    return dt_i * 1027 + tok

                with ExitStack() as X0:
                    xt = X0.enter_context(nc.sbuf_tensor("xt", [128, D], F32))
                    prev_apply = 0
                    for tt in range(9):
                        nrow = 128 if tt < 8 else 4
                        src = x_in[tt * 128:(tt + 1) * 128, :] if tt < 8 else xh_in[0:4, :]
                        W("y", sV, prev_apply)
                        tx = stX.one(xt[0:nrow, :], src)
                        sc = tt * 8
                        m = stats[0:nrow, sc:sc + 1]
                        rs = stats[0:nrow, sc + 1:sc + 2]
                        comb = stats[0:nrow, sc + 2:sc + 3]
                        c1 = stats[0:nrow, sc + 3:sc + 4]
                        c2 = stats[0:nrow, sc + 4:sc + 5]
                        e2 = stats[0:nrow, sc + 5:sc + 6]
                        varp = stats[0:nrow, sc + 6:sc + 7]
                        lncl = stats[0:nrow, sc + 7:sc + 8]
                        W("v", stX.ctr, tx)
                        for hf in range(8):
                            emv("bn_stats", bnst[0:nrow, hf * 6:hf * 6 + 6],
                                xt[0:nrow, hf * 512:hf * 512 + 512])
                        tbn = inc_last("v", sV)
                        W("v", sV, tbn)
                        emv("tensor_mul", smallA[0:nrow, 16:32], bnst[0:nrow, 1:48:3],
                            bnst[0:nrow, 1:48:3])
                        emv("reduce_sum", m, bnst[0:nrow, 1:48:3], axis=AXL.X)
                        emv("tensor_scalar_mul", m, m, 1.0 / 16.0)
                        emv("reduce_sum", e2, bnst[0:nrow, 2:48:3], axis=AXL.X)
                        emv("reduce_sum", varp, smallA[0:nrow, 16:32], axis=AXL.X)
                        emv("scalar_tensor_tensor", e2, varp, 256.0, e2, OP.mult, OP.add)
                        emv("tensor_scalar_mul", e2, e2, 1.0 / D)
                        emv("scalar_tensor_tensor", varp, m, m, e2, OP.mult, OP.subtract)
                        emv("tensor_scalar", varp, varp, -1.0, EPS, OP.mult, OP.add)
                        tv1 = inc_last("v", sV)
                        W("s", sV, tv1)
                        ems("activation", lncl, varp, AF.Ln)
                        ems("activation", rs, lncl, AF.Exp, scale=-0.5, inc=(sS.h, 1))
                        ts1 = sS.tick()
                        W("v", sS, ts1)
                        emv("reciprocal", e2, varp)
                        emv("tensor_scalar", comb, varp, EPS, None, OP.subtract)
                        emv("tensor_mul", comb, comb, e2)
                        emv("tensor_scalar", comb, comb, EPS, None, OP.add)
                        tv2 = inc_last("v", sV)
                        W("s", sV, tv2)
                        ems("activation", lncl, comb, AF.Ln)
                        ems("activation", comb, lncl, AF.Exp, scale=-0.5, inc=(sS.h, 1))
                        ts2 = sS.tick()
                        W("v", sS, ts2)
                        emv("tensor_mul", comb, comb, rs)
                        emv("tensor_scalar", c1, rs, 1.0, None, OP.add)
                        emv("tensor_mul", c2, m, rs)
                        emv("tensor_scalar_mul", c2, c2, -1.0)
                        for dt_i in range(32):
                            slot = dt_i % 8
                            if slot == 0:
                                W("v", sPE, pool.last_pe)
                            xs = xh8[0:nrow, slot * 128:slot * 128 + 128]
                            emv("tensor_scalar", xs, xt[0:nrow, dt_i * 128:dt_i * 128 + 128],
                                m, comb, OP.subtract, OP.mult,
                                inc=(sV.h, 1) if slot == 7 else None)
                            if slot == 7:
                                tap = sV.tick()
                                W("t", sV, tap)

                            def mk_ev(_dt=dt_i, _tt=tt):
                                def ev(ps):
                                    if _tt < 8:
                                        emv("tensor_scalar_mul",
                                            hT[:, hcol(_dt, _tt * 128):hcol(_dt, _tt * 128) + 128],
                                            ps, nw_sb[:, _dt:_dt + 1])
                                    else:
                                        emv("tensor_scalar_mul",
                                            hT[:, hcol(_dt, 1024):hcol(_dt, 1024) + 3],
                                            ps[:, 0:3], nw_sb[:, _dt:_dt + 1])
                                return ev
                            pool.add(xs, mk_ev())
                        pool.flush()
                        prev_apply = sV.n
                    hT_ready = pool.drain()

                if dbg_t("hT", [128, 32 * 1027], BF16):
                    W("y", sV, sV.n)
                    stMISC.one(dbg_outs["hT"][:, :], hT[:])
                if dbg_t("stats", [128, 80]):
                    W("y", sV, sV.n)
                    W("y", sS, sS.n)
                    stMISC.one(dbg_outs["stats"][:, :], stats[:])
                mark("P0")

                blks = [("dt", 0)] + [("bc", j) for j in range(16)] + [("hid", j) for j in range(64)]
                mm_banks = [pA, pB, pC]
                bank_free = {id(pA): 0, id(pB): 0, id(pC): 0}
                halo_free_s = 0
                pE_first = True
                w1_used = {}
                st_bank_free = [0, 0]
                bct_used_pe = 0
                bct_dma = 0
                hst_dma = [0, 0]
                hst_states = [0, 0]
                bd_done = [0, 0]
                gcnt = 0
                pend_states = []

                def emit_states(b_, hpar_, tbd_):
                    g_ = b_ // 8
                    W("t", sV, tbd_)
                    for c in range(NCH):
                        sl = c % 2
                        sbank = pD if sl == 0 else pF
                        W("t", sV, st_bank_free[sl])
                        for hl in (0, 1):
                            emt("matmul", sbank[:, hl * 64:hl * 64 + 64],
                                bdb[:, (hpar_ * 16 + c * 2 + hl) * 128:(hpar_ * 16 + c * 2 + hl) * 128 + 128],
                                hstage[:, hpar_ * 1024 + c * 128 + hl * 64:hpar_ * 1024 + c * 128 + hl * 64 + 64],
                                start=(hl == 0), stop=(hl == 1),
                                inc=(sPE.h, 1) if hl == 1 else None)
                        smm = sPE.tick()
                        W("v", sPE, smm)
                        for hl in (0, 1):
                            hh = 2 * b_ + hl
                            emv("scalar_tensor_tensor", Rbuf[:, hh * 64:hh * 64 + 64],
                                Rbuf[:, hh * 64:hh * 64 + 64],
                                dtotb[:, c * 128 + hh:c * 128 + hh + 1],
                                sbank[:, hl * 64:hl * 64 + 64], OP.mult, OP.add,
                                inc=(sV.h, 1) if hl == 1 else None)
                        st_bank_free[sl] = sV.tick()
                    hst_states[hpar_] = sPE.n

                W("t", sV, hT_ready)

                if p1_limit is not None:
                    blks = blks[:p1_limit]
                for bi, (kind, j) in enumerate(blks):
                    if kind == "dt":
                        c0 = DT0
                    elif kind == "bc":
                        c0 = HBC0 + (64 + j) * 128
                    else:
                        c0 = HBC0 + j * 128
                    par = bi % 2
                    wb = w1buf[:, par * 4096:(par + 1) * 4096]
                    if bi >= 2:
                        W("g", sPE, w1_used[bi - 2])
                    _s = stW1 if par == 0 else stW1b
                    tw = _s.one(wb.rearrange("p (a o) -> p a o", o=128),
                                w1t[:, c0:c0 + 128].rearrange("(a p) o -> p a o", p=128))
                    W("t", _s.ctr, tw)
                    ngrp = 2 if kind == "dt" else 3
                    grp_bank = []
                    for grp in range(ngrp):
                        if grp == 2:
                            bank = pE_
                            W("t", sS, halo_free_s)
                            if pE_first:
                                W("t", sV, sV.n)
                                pE_first = False
                        else:
                            bank = mm_banks[gcnt % 3]
                            gcnt += 1
                            W("t", sS, bank_free[id(bank)])
                        grp_bank.append(bank)
                        for dt_i in range(32):
                            if grp < 2:
                                o = bank[:, 0:512]
                                rhs = hT[:, hcol(dt_i, grp * 512):hcol(dt_i, grp * 512) + 512]
                            else:
                                o = bank[:, 0:3]
                                rhs = hT[:, hcol(dt_i, 1024):hcol(dt_i, 1024) + 3]
                            emt("matmul", o, wb[:, dt_i * 128:dt_i * 128 + 128], rhs,
                                start=(dt_i == 0), stop=(dt_i == 31),
                                inc=(sPE.h, 1) if dt_i == 31 else None)
                        sPE.tick()
                    w1_used[bi] = sPE.n

                    W("s", sPE, sPE.n)
                    if kind == "dt":
                        ems("copy", scr[:, 0:512], grp_bank[0][:, 0:512])
                        ems("copy", scr[:, 512:1024], grp_bank[1][:, 0:512], inc=(sS.h, 1))
                        tev = sS.tick()
                        bank_free[id(grp_bank[0])] = bank_free[id(grp_bank[1])] = tev
                        if dbg_t("dtraw", [128, 1024]):
                            W("y", sS, tev)
                            stMISC.one(dbg_outs["dtraw"][:, :], scr[:, 0:1024])
                        mark("P1mm")
                        dtraw = scr[:, 0:1024]
                        wk1 = scr[:, 1024:2048]
                        wk2 = scr[:, 2048:3072]
                        ems("activation", wk1, dtraw, AF.Abs, bias=dtb_sb[:, 0:1])
                        ems("activation", wk2, wk1, AF.Exp, scale=-1.0)
                        ems("activation", wk1, wk2, AF.Ln, bias=1.0, inc=(sS.h, 1))
                        tch = sS.tick()
                        W("v", sS, tch)
                        emv("tensor_scalar", dt_sb[:], dtraw, dtb_sb[:, 0:1], 0.0, OP.add, OP.max)
                        emv("tensor_add", dt_sb[:], dt_sb[:], wk1)
                        emv("tensor_scalar_mul", wk2, dt_sb[:], acol[:, 0:1])
                        for c in range(NCH):
                            emv("tensor_tensor_scan", acs[:, c * 128:(c + 1) * 128],
                                wk2[:, c * 128:(c + 1) * 128], wk2[:, c * 128:(c + 1) * 128],
                                0.0, OP.add, OP.bypass)
                        emv("tensor_copy", aclast[:], acs[:, 127:T:128])
                        emv("reduce_sum", logdt[:], aclast[:], axis=AXL.X)
                        tdt = inc_last("v", sV)
                        if dbg_t("acs", [128, T]):
                            W("y", sV, tdt)
                            stMISC.one(dbg_outs["acs"][:, :], acs[:])
                        mark("P1sc")
                        W("t", sV, tdt)
                        for c in range(NCH):
                            if c > 0:
                                W("t", sV, sV.n)
                                W("t", sS, sS.n)
                            emt("transpose", pF[:, 0:128], acs[:, c * 128:(c + 1) * 128], identf[:])
                            emt("transpose", pF[:, 128:256], dt_sb[:, c * 128:(c + 1) * 128], identf[:],
                                inc=(sPE.h, 1))
                            t1 = sPE.tick()
                            W("v", sPE, t1)
                            emv("tensor_copy", dtT[:, c * 128:(c + 1) * 128], pF[:, 128:256])
                            emv("tensor_copy", ddt[:, c * 128:(c + 1) * 128], pF[:, 0:128])
                            t2 = inc_last("v", sV)
                            W("t", sV, t2)
                            emt("matmul", pD[0:1, 0:128], aclast[:, c:c + 1], identf[:],
                                start=True, stop=True, inc=(sPE.h, 1))
                            t3 = sPE.tick()
                            W("v", sPE, t3)
                            emv("tensor_copy", rowbuf[0:1, :], pD[0:1, 0:128], inc=(sV.h, 1))
                            t4 = sV.tick()
                            W("t", sV, t4)
                            emt("matmul", pD[:, 0:128], ones1[0:1, :], rowbuf[0:1, :],
                                start=True, stop=True, inc=(sPE.h, 1))
                            t5 = sPE.tick()
                            W("s", sPE, t5)
                            ems("activation", dtotb[:, c * 128:(c + 1) * 128], pD[:, 0:128], AF.Exp,
                                inc=(sS.h, 1))
                            t6 = sS.tick()
                            W("v", sPE, t5)
                            W("v", sS, t6)
                            emv("tensor_sub", wk1[:, 0:128], pD[:, 0:128], ddt[:, c * 128:(c + 1) * 128])
                            t7 = inc_last("v", sV)
                            W("s", sV, t7)
                            ems("activation", wk2[:, 0:128], wk1[:, 0:128], AF.Exp, inc=(sS.h, 1))
                            t8 = sS.tick()
                            W("v", sS, t8)
                            emv("tensor_mul", ddt[:, c * 128:(c + 1) * 128], wk2[:, 0:128],
                                dtT[:, c * 128:(c + 1) * 128], inc=(sV.h, 1))
                            sV.tick()
                        st_bank_free = [sV.n, sV.n]
                        halo_free_s = sS.n
                        if dbg_t("dt", [128, T]):
                            W("y", sV, sV.n)
                            stMISC.one(dbg_outs["dt"][:, :], dt_sb[:])
                        if dbg_t("acs", [128, T]):
                            W("y", sV, sV.n)
                            stMISC.one(dbg_outs["acs"][:, :], acs[:])
                        if dbg_t("ddt", [128, T]):
                            W("y", sV, sV.n)
                            stMISC.one(dbg_outs["ddt"][:, :], ddt[:])
                        if dbg_t("dtotb", [128, T]):
                            W("y", sV, sV.n)
                            stMISC.one(dbg_outs["dtotb"][:, :], dtotb[:])
                        continue

                    ems("copy", hbcraw[:, 3:515], grp_bank[0][:, 0:512])
                    ems("copy", hbcraw[:, 515:1027], grp_bank[1][:, 0:512])
                    ems("copy", hbcraw[:, 0:3], grp_bank[2][:, 0:3], inc=(sS.h, 1))
                    tev = sS.tick()
                    bank_free[id(grp_bank[0])] = bank_free[id(grp_bank[1])] = tev
                    halo_free_s = tev
                    cblk = (64 + j) if kind == "bc" else j
                    W("v", sS, tev)
                    emv("tensor_scalar", scr[:, 0:1024], hbcraw[:, 3:1027],
                        cw_sb[:, cblk * 4 + 3:cblk * 4 + 4], cb_sb[:, cblk:cblk + 1], OP.mult, OP.add)
                    for i in (2, 1, 0):
                        emv("scalar_tensor_tensor", scr[:, 0:1024], hbcraw[:, i:i + 1024],
                            cw_sb[:, cblk * 4 + i:cblk * 4 + i + 1], scr[:, 0:1024], OP.mult, OP.add)
                    tconv = inc_last("v", sV)
                    W("s", sV, tconv)

                    if kind == "bc":
                        W("s", stST.ctr, bct_dma)
                        W("s", sPE, bct_used_pe)
                        ems("activation", bct[:, 0:1024], scr[:, 0:1024], AF.Silu, inc=(sS.h, 1))
                        tsil = sS.tick()
                        W("y", sS, tsil)
                        bct_dma = stST.one(bc_dd[j, :, :], bct[:, 0:1024])
                        if j < 8:
                            W("t", sS, tsil)
                            for c in range(NCH):
                                def mk_ev2(_c=c, _g=j):
                                    def ev(ps):
                                        emv("tensor_copy",
                                            btok[:, (_c * 8 + _g) * 128:(_c * 8 + _g) * 128 + 128], ps)
                                    return ev
                                pool.add(bct[:, c * 128:(c + 1) * 128], mk_ev2())
                            pool.flush()
                            bct_used_pe = pool.last_pe
                        continue

                    b = j
                    g = b // 8
                    hpar = b % 2
                    ems("activation", hfeat[:, 0:1024], scr[:, 0:1024], AF.Silu, inc=(sS.h, 1))
                    tsil = sS.tick()
                    if pend_states:
                        emit_states(*pend_states.pop(0))
                    W("v", stST.ctr, hst_dma[hpar])
                    W("v", sPE, hst_states[hpar])
                    W("t", sS, tsil)
                    for c in range(NCH):
                        def mk_ev3(_c=c, _hp=hpar):
                            def ev(ps):
                                emv("tensor_copy",
                                    hstage[:, _hp * 1024 + _c * 128:_hp * 1024 + (_c + 1) * 128], ps)
                            return ev
                        pool.add(hfeat[:, c * 128:(c + 1) * 128], mk_ev3())
                    pool.flush()
                    thid = sV.n
                    W("y", sV, thid)
                    hst_dma[hpar] = stST.one(
                        hidT_d[:, :, b * 128:(b + 1) * 128].rearrange("c t o -> t c o"),
                        hstage[:, hpar * 1024:hpar * 1024 + 1024].rearrange("t (c o) -> t c o", o=128))
                    W("v", sPE, hst_states[hpar])
                    for c in range(NCH):
                        for hl in (0, 1):
                            hh = 2 * b + hl
                            emv("tensor_scalar_mul",
                                bdb[:, (hpar * 16 + c * 2 + hl) * 128:(hpar * 16 + c * 2 + hl) * 128 + 128],
                                btok[:, (c * 8 + g) * 128:(c * 8 + g) * 128 + 128],
                                ddt[:, c * 128 + hh:c * 128 + hh + 1])
                    tbd = inc_last("v", sV)
                    pend_states.append((b, hpar, tbd))

                while pend_states:
                    emit_states(*pend_states.pop(0))
                pool.drain()
                p1_v = sV.n

                if dbg_t("Rloc", [128, H * PD]):
                    W("y", sV, p1_v)
                    stMISC.one(dbg_outs["Rloc"][:, :], Rbuf[:])
                if dbg_t("btok", [128, 64 * 128], BF16):
                    W("y", sV, p1_v)
                    stMISC.one(dbg_outs["btok"][:, :], btok[:])
                mark("P1")

                W("g", sV, p1_v)
                for hh in range(H):
                    stPK.dma(ccs_in[hh, :].rearrange("(n p) -> n p", n=NS),
                             Rbuf[:, hh * 64:hh * 64 + 64])
                pack_t = stPK.close()
                W("y", sV, p1_v)
                ccl_t = stSI.one(ccl_in[:, 0:1], logdt[:])
                W("g", stPK.ctr, pack_t)
                W("g", stSI.ctr, ccl_t)
                emg("collective_compute", "AllGather", OP.bypass, replica_groups=RG,
                    ins=[ccs_in[:, :].opt()], outs=[ccs_out[:, :].opt()], inc=(sCC.h, 1))
                sCC.tick()
                emg("collective_compute", "AllGather", OP.bypass, replica_groups=RG,
                    ins=[ccl_in[:, :].opt()], outs=[ccl_out[:, :].opt()], inc=(sCC.h, 1))
                cc_done = sCC.tick()
                mark("CC")

                gate_dmas = {}
                for b in range(64):
                    c0 = b * 128
                    bi = len(blks) + b
                    par = bi % 2
                    wb = w1buf[:, par * 4096:(par + 1) * 4096]
                    W("g", sPE, w1_used.get(bi - 2, 0))
                    _s = stW1 if par == 0 else stW1b
                    tw = _s.one(wb.rearrange("p (a o) -> p a o", o=128),
                                w1t[:, c0:c0 + 128].rearrange("(a p) o -> p a o", p=128))
                    W("t", _s.ctr, tw)
                    gpar = b % 2
                    for grp in range(2):
                        bank = mm_banks[gcnt % 3]
                        gcnt += 1
                        W("t", sS, bank_free[id(bank)])
                        for dt_i in range(32):
                            emt("matmul", bank[:, 0:512], wb[:, dt_i * 128:dt_i * 128 + 128],
                                hT[:, hcol(dt_i, grp * 512):hcol(dt_i, grp * 512) + 512],
                                start=(dt_i == 0), stop=(dt_i == 31),
                                inc=(sPE.h, 1) if dt_i == 31 else None)
                        sPE.tick()
                        w1_used[bi] = sPE.n
                        W("s", sPE, sPE.n)
                        if grp == 0 and b >= 2:
                            W("s", stGT.ctr, gate_dmas[b - 2])
                        ems("activation", gatef[:, gpar * 1024 + grp * 512:gpar * 1024 + grp * 512 + 512],
                            bank[:, 0:512], AF.Silu, inc=(sS.h, 1))
                        bank_free[id(bank)] = sS.tick()
                    W("y", sS, sS.n)
                    gate_dmas[b] = stGT.one(gate_dd[b, :, :], gatef[:, gpar * 1024:gpar * 1024 + 1024])
                mark("P2")
                barrier()

            with ExitStack() as FX:
                EF = FX.enter_context
                at_sb = EF(nc.sbuf_tensor("at_sb", [128, 32 * 1024], BF16))
                wrt_sb = EF(nc.sbuf_tensor("wrt_sb", [128, 32 * 64], BF16))
                xre = EF(nc.sbuf_tensor("xre", [128, 2 * 2048], F32))
                xnt = EF(nc.sbuf_tensor("xnt", [128, 2 * 4096], BF16))
                xnb = EF(nc.sbuf_tensor("xnb", [128, 8 * 128], BF16))
                hgb = EF(nc.sbuf_tensor("hgb", [128, 1024], BF16))
                hgT = EF(nc.sbuf_tensor("hgT", [128, 64 * 128], BF16))
                scoreb = EF(nc.sbuf_tensor("scoreb", [128, 160], F32))
                bmb = EF(nc.sbuf_tensor("bmb", [128, 8 * 512], BF16))
                ffst = EF(nc.sbuf_tensor("ffst", [128, 4 * 512], BF16))

                tat = stWB.one(at_sb[:].rearrange("p (a k) -> p a k", k=1024),
                               at_d.rearrange("(a p) k -> p a k", p=128))
                twr = stWB.one(wrt_sb[:].rearrange("p (a k) -> p a k", k=64),
                               wrt_d.rearrange("(a p) k -> p a k", p=128))

                hffn_free = 0
                score_free = 0
                xnt_users = [0, 0]
                xre_users = [0, 0]
                txn_t = {}
                xld = {0: stX.one(xre[:, 0:2048], x_in[0:128, 0:2048])}
                for tt in range(9):
                    if tt >= 1:
                        pt = tt - 1
                        xsl = pt % 2
                        base = xsl * 4096
                        sbof = 0
                        W("t", sV, txn_t[pt])
                        W("t", stWB.ctr, twr)
                        if score_free:
                            W("t", sV, score_free)
                        for dt_i in range(32):
                            emt("matmul", pE_[:, 0:64], xnt[:, base + dt_i * 128:base + (dt_i + 1) * 128],
                                wrt_sb[:, dt_i * 64:(dt_i + 1) * 64], start=(dt_i == 0), stop=(dt_i == 31),
                                inc=(sPE.h, 1) if dt_i == 31 else None)
                        tscm = sPE.tick()
                        if hffn_free:
                            W("t", sV, hffn_free)
                        hmm = 0
                        for krg in range(2):
                            bank = pA if krg == 0 else pB
                            for dt_i in range(32):
                                emt("matmul", bank[:, 0:512], xnt[:, base + dt_i * 128:base + (dt_i + 1) * 128],
                                    at_sb[:, dt_i * 1024 + krg * 512:dt_i * 1024 + krg * 512 + 512],
                                    start=(dt_i == 0), stop=(dt_i == 31),
                                    inc=(sPE.h, 1) if dt_i == 31 else None)
                            hmm = sPE.tick()
                        xnt_users[xsl] = sPE.n
                        W("v", sPE, tscm)
                        emv("tensor_copy", scoreb[:, sbof + 0:sbof + 64], pE_[:, 0:64])
                        emv("max", scoreb[:, sbof + 64:sbof + 72], scoreb[:, sbof + 0:sbof + 64])
                        emv("match_replace", scoreb[:, sbof + 80:sbof + 144],
                            scoreb[:, sbof + 64:sbof + 72], scoreb[:, sbof + 0:sbof + 64], NEG)
                        emv("max", scoreb[:, sbof + 72:sbof + 80], scoreb[:, sbof + 80:sbof + 144])
                        emv("tensor_scalar_mul", scoreb[:, sbof + 144:sbof + 145],
                            scoreb[:, sbof + 64:sbof + 65], -1.0)
                        score_free = inc_last("v", sV)
                        W("s", sV, score_free)
                        ems("activation", scoreb[:, sbof + 80:sbof + 144], scoreb[:, sbof + 0:sbof + 64],
                            AF.Exp, bias=scoreb[:, sbof + 144:sbof + 145], inc=(sS.h, 1))
                        tg2 = sS.tick()
                        W("v", sS, tg2)
                        emv("tensor_scalar", scoreb[:, sbof + 0:sbof + 64], scoreb[:, sbof + 0:sbof + 64],
                            scoreb[:, sbof + 79:sbof + 80], None, OP.is_ge)
                        emv("tensor_mul", scoreb[:, sbof + 80:sbof + 144],
                            scoreb[:, sbof + 80:sbof + 144], scoreb[:, sbof + 0:sbof + 64])
                        emv("reduce_sum", scoreb[:, sbof + 145:sbof + 146],
                            scoreb[:, sbof + 80:sbof + 144], axis=AXL.X)
                        emv("reciprocal", scoreb[:, sbof + 146:sbof + 147],
                            scoreb[:, sbof + 145:sbof + 146])
                        emv("tensor_scalar", scoreb[:, sbof + 80:sbof + 144],
                            scoreb[:, sbof + 80:sbof + 144], scoreb[:, sbof + 146:sbof + 147], FFSCALE,
                            OP.mult, OP.mult)
                        W("v", sPE, hmm)
                        W("v", sPE, pool.last_pe)
                        for k in range(64):
                            bank = pA if k < 32 else pB
                            emv("tensor_scalar_mul", hgb[:, k * 16:(k + 1) * 16],
                                bank[:, (k % 32) * 16:(k % 32) * 16 + 16],
                                scoreb[:, sbof + 80 + k:sbof + 81 + k])
                        hffn_free = inc_last("v", sV)
                        W("t", sV, hffn_free)
                        for krt in range(8):
                            def mk_ev5(_tt=pt, _krt=krt):
                                def ev(ps):
                                    emv("tensor_copy",
                                        hgT[:, (_tt * 8 + _krt) * 128:(_tt * 8 + _krt) * 128 + 128], ps)
                                return ev
                            pool.add(hgb[:, krt * 128:(krt + 1) * 128], mk_ev5())
                        pool.flush()
                    if tt < 8:
                        sc = tt * 8
                        m = stats[:, sc:sc + 1]
                        rs = stats[:, sc + 1:sc + 2]
                        xsl_t = tt % 2
                        for half in range(2):
                            k = tt * 2 + half
                            nk = k + 1
                            if nk < 16:
                                ntt, nhalf = nk // 2, nk % 2
                                W("y", sV, xre_users[nk % 2])
                                xld[nk] = stX.one(xre[:, (nk % 2) * 2048:(nk % 2 + 1) * 2048],
                                                  x_in[ntt * 128:(ntt + 1) * 128,
                                                       nhalf * 2048:(nhalf + 1) * 2048])
                            W("v", stX.ctr, xld[k])
                            if half == 0:
                                W("v", sPE, xnt_users[xsl_t])
                            tap = 0
                            for qq in range(16):
                                dt_i = half * 16 + qq
                                slot = dt_i % 8
                                if slot == 0:
                                    W("v", sPE, pool.last_pe)
                                xs = xnb[:, slot * 128:slot * 128 + 128]
                                emv("tensor_scalar", xs,
                                    xre[:, (k % 2) * 2048 + qq * 128:(k % 2) * 2048 + qq * 128 + 128],
                                    m, rs, OP.subtract, OP.mult,
                                    inc=(sV.h, 1) if slot == 7 else None)
                                if slot == 7:
                                    tap = sV.tick()
                                    W("t", sV, tap)

                                def mk_ev4(_dt=dt_i, _b=xsl_t * 4096):
                                    def ev(ps):
                                        emv("tensor_copy", xnt[:, _b + _dt * 128:_b + (_dt + 1) * 128], ps)
                                    return ev
                                pool.add(xs, mk_ev4())
                            pool.flush()
                            xre_users[k % 2] = tap
                        txn_t[tt] = pool.drain()

                hgT_ready = pool.drain()

                if dbg_t("g2", [128, 64]):
                    W("y", sV, sV.n)
                    stMISC.one(dbg_outs["g2"][:, :], scoreb[:, 80:144])
                if dbg_t("hgT", [128, 64 * 128], BF16):
                    W("y", sV, sV.n)
                    stMISC.one(dbg_outs["hgT"][:, :], hgT[:])

                ob_free = {id(pC): 0, id(pD): 0}
                bm_used = 0
                ffs_v = {}
                for mg in range(8):
                    W("g", sPE, bm_used)
                    for krt in range(8):
                        stWB.dma(bmb[:, krt * 512:(krt + 1) * 512],
                                 bm_d[krt * 128:(krt + 1) * 128, mg * 512:(mg + 1) * 512])
                    tbm = stWB.close()
                    for tokt in range(8):
                        bank = pC if tokt % 2 == 0 else pD
                        W("t", sV, ob_free[id(bank)])
                        if tokt == 0:
                            W("t", sV, hgT_ready)
                            W("t", stWB.ctr, tbm)
                        for krt in range(8):
                            emt("matmul", bank[:, 0:512],
                                hgT[:, (tokt * 8 + krt) * 128:(tokt * 8 + krt) * 128 + 128],
                                bmb[:, krt * 512:(krt + 1) * 512],
                                start=(krt == 0), stop=(krt == 7),
                                inc=(sPE.h, 1) if krt == 7 else None)
                        tmm = sPE.tick()
                        bm_used = tmm
                        W("v", sPE, tmm)
                        fslot = tokt % 4
                        if mg > 0 and tokt == 0:
                            W("v", stFS.ctr, ffs_b)
                        if tokt == 4:
                            W("v", stFS.ctr, ffs_a)
                        emv("tensor_copy", ffst[:, fslot * 512:(fslot + 1) * 512], bank[:, 0:512],
                            inc=(sV.h, 1))
                        ob_free[id(bank)] = sV.tick()
                        ffs_v[tokt] = ob_free[id(bank)]
                        W("y", sV, ffs_v[tokt])
                        stFS.dma(ffn_dd[tokt * 128:(tokt + 1) * 128, mg * 512:(mg + 1) * 512],
                                 ffst[:, fslot * 512:(fslot + 1) * 512])
                        if tokt == 3:
                            ffs_a = stFS.close()
                    ffs_b = stFS.close()
                mark("P25")
                barrier()

            with ExitStack() as X3:
                E3 = X3.enter_context
                acsT = E3(nc.sbuf_tensor("acsT", [128, T], F32))
                cbt = E3(nc.sbuf_tensor("cbt", [128, 8 * 128], F32))
                e4 = E3(nc.sbuf_tensor("e4", [128, 2 * 512], BF16))
                hidc = E3(nc.sbuf_tensor("hidc", [128, DIN], BF16))
                gTc = E3(nc.sbuf_tensor("gTc", [128, DIN], BF16))
                ygc = E3(nc.sbuf_tensor("ygc", [128, DIN], BF16))
                bcc = E3(nc.sbuf_tensor("bcc", [128, 16 * 128], BF16))
                ygst = E3(nc.sbuf_tensor("ygst", [128, DIN], BF16))
                gf8 = E3(nc.sbuf_tensor("gf8", [128, 8 * 128], BF16))
                rbb = E3(nc.sbuf_tensor("rbb", [128, 2 * 512], F32))
                rot = E3(nc.sbuf_tensor("rot", [128, 4 * 128], F32))
                texb = E3(nc.sbuf_tensor("texb", [128, 4 * 128], BF16))
                rotb = E3(nc.sbuf_tensor("rotb", [128, 1920], BF16))
                sinw = E3(nc.sbuf_tensor("sinw", [128, 32], F32))
                gns = E3(nc.sbuf_tensor("gns", [128, 24], F32))
                bnst3 = E3(nc.sbuf_tensor("bnst3", [128, 96], F32))
                gsc = E3(nc.sbuf_tensor("gsc", [128, 64], F32))

                for c in range(NCH):
                    if c > 0:
                        W("t", sV, sV.n)
                    emt("transpose", pF[:, 0:128], acs[:, c * 128:(c + 1) * 128], identf[:],
                        inc=(sPE.h, 1))
                    tt1 = sPE.tick()
                    W("v", sPE, tt1)
                    emv("tensor_copy", acsT[:, c * 128:(c + 1) * 128], pF[:, 0:128], inc=(sV.h, 1))
                    sV.tick()

                W("y", sCC, cc_done)
                for jj in range(4):
                    stSI.dma(sinw[:, jj:jj + 1], ccl_out[jj * 128:(jj + 1) * 128, 0:1])
                lw_t = stSI.close()
                W("v", stSI.ctr, lw_t)
                emv("tensor_mul", sinw[:, 4:8], sinw[:, 0:4], selm[:, 0:4])
                emv("tensor_tensor_scan", sinw[:, 8:12], sinw[:, 4:8], sinw[:, 4:8], 0.0,
                    OP.add, OP.bypass)
                emv("reduce_sum", sinw[:, 12:13], sinw[:, 4:8], axis=AXL.X)
                emv("tensor_scalar", sinw[:, 16:20], sinw[:, 8:12], sinw[:, 12:13], -1.0,
                    OP.subtract, OP.mult)
                tw1 = inc_last("v", sV)
                W("s", sV, tw1)
                ems("activation", sinw[:, 20:24], sinw[:, 16:20], AF.Exp, inc=(sS.h, 1))
                tw2 = sS.tick()
                W("v", sS, tw2)
                emv("tensor_mul", sinw[:, 20:24], sinw[:, 20:24], selm[:, 0:4])
                emv("memset", Rbuf[:], 0.0, inc=(sV.h, 1))
                sV.tick()
                for jj in range(4):
                    W("y", sV, sV.n)
                    tld = stSI.one(hidc[:, 0:DIN], ccs_out[jj * 128:(jj + 1) * 128, :])
                    W("v", stSI.ctr, tld)
                    emv("scalar_tensor_tensor", Rbuf[:], hidc[:, 0:DIN], sinw[:, 20 + jj:21 + jj],
                        Rbuf[:], OP.mult, OP.add, inc=(sV.h, 1))
                    sV.tick()
                W("g", sV, sV.n)
                tso = stPK.one(sin_dd[:, :], Rbuf[:])
                W("y", stPK.ctr, tso)
                for hh in range(H):
                    stSI.dma(Rbuf[:, hh * 64:hh * 64 + 64], sin_dd[hh, :].rearrange("(n p) -> n p", n=NS))
                tsin = stSI.close()
                W("v", stSI.ctr, tsin)
                emv("nop", inc=(sV.h, 1))
                r2_ready = sV.tick()
                if dbg_t("Sin", [128, H * PD]):
                    W("y", sV, r2_ready)
                    stMISC.one(dbg_outs["Sin"][:, :], Rbuf[:])

                hid_users_pe = 0
                hid_users_v = r2_ready
                ygst_dma = 0
                yb_free = [0, 0]
                s2_free = [0, 0]
                rb_free = [(0, 0), (0, 0)]
                cb_bank_v = {id(pA): 0, id(pB): 0, id(pC): 0}
                frow_t = [0, 0]
                rb_t = {}
                texp_t = {}
                mstt_t = {}
                prep2_t = {}
                prep2g_t = {}
                prep2s_t = {}
                r2w_t = {hh_: r2_ready for hh_ in range(H)}
                ymm_t = {}
                s2mm_t = {}
                e4_user = [0, 0]

                hid_users_g = 0
                for c in range(NCH):
                    W("y", sPE, hid_users_pe)
                    W("y", sV, hid_users_v)
                    W("y", sG, hid_users_g)
                    stCH.dma(hidc[:, 0:DIN], hidT_d[c, :, :])
                    for tcol in range(16):
                        stCH.dma(bcc[:, tcol * 128:(tcol + 1) * 128],
                                 bc_dd[tcol, :, c * 128:(c + 1) * 128])
                    stCH.dma(flat_dd[c, :].rearrange("(hh i) -> hh i", hh=128),
                             acs[:, c * 128:(c + 1) * 128])
                    tch = stCH.close()
                    for ob in range(64):
                        slot = ob % 8
                        if slot == 0:
                            W("y", sPE, pool.last_pe)
                        stGF.dma(gf8[:, slot * 128:slot * 128 + 128],
                                 gate_dd[ob, :, c * 128:(c + 1) * 128])
                        if slot == 7:
                            tg = stGF.close()
                            W("t", stGF.ctr, tg)

                        def mk_ev6(_ob=ob):
                            def ev(ps):
                                emv("tensor_copy", gTc[:, _ob * 128:(_ob + 1) * 128], ps)
                            return ev
                        pool.add(gf8[:, slot * 128:slot * 128 + 128], mk_ev6())
                    pool.drain()
                    W("t", stCH.ctr, tch)
                    for gg in range(8):
                        bank = mm_banks[gg % 3]
                        W("t", sV, cb_bank_v[id(bank)])
                        if gg == 2:
                            W("t", sV, s2_free[1])
                        emt("matmul", bank[:, 0:128], bcc[:, gg * 128:(gg + 1) * 128],
                            bcc[:, (8 + gg) * 128:(8 + gg) * 128 + 128],
                            start=True, stop=True, inc=(sPE.h, 1))
                        tcb = sPE.tick()
                        W("v", sPE, tcb)
                        emv("tensor_mul", cbt[:, gg * 128:(gg + 1) * 128], bank[:, 0:128], triu[:],
                            inc=(sV.h, 1))
                        cb_bank_v[id(bank)] = sV.tick()

                    W("v", stCH.ctr, tch)

                    def stage_rb(hh):
                        hg4 = hh // 4
                        fsl = hg4 % 2
                        if hh % 4 == 0:
                            W("y", sV, rb_free[fsl][0])
                            W("y", sS, rb_free[fsl][1])
                            W("y", stCH.ctr, tch)
                            sl_ = flat_dd[c, hg4 * 512:(hg4 + 1) * 512]
                            bap = _APC(tensor=sl_.tensor, offset=sl_.offset,
                                       ap=[[0, 128], [1, 512]])
                            stFR.dma(rbb[:, fsl * 512:(fsl + 1) * 512], bap)
                            rb_t[(c * 32) + hg4] = stFR.close()
                            W("s", stFR.ctr, rb_t[(c * 32) + hg4])
                            W("s", sG, e4_user[fsl])
                            ems("activation", e4[:, fsl * 512:(fsl + 1) * 512],
                                rbb[:, fsl * 512:(fsl + 1) * 512], AF.Exp, inc=(sS.h, 1))
                            rb_free[fsl] = (rb_free[fsl][0], sS.tick())

                    for it_ in range(H + 3):
                        if it_ < H:
                            hh = it_
                            ghh = c * H + hh
                            stage_rb(hh)
                            hg4 = hh // 4
                            fsl = hg4 % 2
                            s4 = hh % 4
                            W("v", stFR.ctr, rb_t[(c * 32) + hg4])
                            W("v", sS, texp_t.get(ghh - 4, 0))
                            emv("tensor_scalar", rot[:, s4 * 128:s4 * 128 + 128],
                                rbb[:, fsl * 512 + (hh % 4) * 128:fsl * 512 + (hh % 4) * 128 + 128],
                                acsT[:, c * 128 + hh:c * 128 + hh + 1], 0.0,
                                OP.subtract, OP.min, inc=(sV.h, 1))
                            t0t = sV.tick()
                            rb_free[fsl] = (t0t, rb_free[fsl][1])
                            W("s", sV, t0t)
                            W("s", sV, mstt_t.get(ghh - 4, 0))
                            ems("activation", texb[:, s4 * 128:s4 * 128 + 128],
                                rot[:, s4 * 128:s4 * 128 + 128], AF.Exp, inc=(sS.h, 1))
                            texp_t[ghh] = sS.tick()
                        if 1 <= it_ <= H:
                            hh = it_ - 1
                            ghh = c * H + hh
                            gg = hh // 16
                            hg4 = hh // 4
                            fsl = hg4 % 2
                            s4 = hh % 4
                            W("v", sS, texp_t[ghh])
                            W("v", sPE, ymm_t.get(ghh - 4, 0))
                            W("v", sPE, s2mm_t.get(ghh - 4, 0))
                            emv("scalar_tensor_tensor", rotb[:, s4 * 128:s4 * 128 + 128],
                                texb[:, s4 * 128:s4 * 128 + 128],
                                dtT[:, c * 128 + hh:c * 128 + hh + 1],
                                cbt[:, gg * 128:(gg + 1) * 128], OP.mult, OP.mult, inc=(sV.h, 1))
                            mstt_t[ghh] = sV.tick()
                            W("g", sPE, ymm_t.get(ghh - 4, 0))
                            W("g", sPE, s2mm_t.get(ghh - 4, 0))
                            W("g", sS, rb_free[fsl][1])
                            emg("tensor_mul", rotb[:, 512 + s4 * 128:512 + s4 * 128 + 128],
                                bcc[:, (8 + gg) * 128:(8 + gg) * 128 + 128],
                                e4[:, fsl * 512 + (hh % 4) * 128:fsl * 512 + (hh % 4) * 128 + 128],
                                inc=(sG.h, 1))
                            e4_user[fsl] = sG.tick()
                            W("g", sV, r2w_t.get(hh, 0))
                            emg("tensor_copy", rotb[:, 1536 + s4 * 64:1536 + s4 * 64 + 64],
                                Rbuf[:, hh * 64:hh * 64 + 64], inc=(sG.h, 1))
                            prep2g_t[ghh] = sG.tick()
                            hid_users_g = prep2g_t[ghh]
                            W("s", sPE, ymm_t.get(ghh - 4, 0))
                            W("s", sPE, s2mm_t.get(ghh - 4, 0))
                            ems("activation", rotb[:, 1024 + s4 * 128:1024 + s4 * 128 + 128],
                                btok[:, (c * 8 + gg) * 128:(c * 8 + gg) * 128 + 128],
                                AF.Copy, scale=ddt[:, c * 128 + hh:c * 128 + hh + 1],
                                inc=(sS.h, 1))
                            prep2s_t[ghh] = sS.tick()
                        if it_ >= 3 and it_ - 3 < H:
                            hh = it_ - 3
                            ghh = c * H + hh
                            sl = hh % 2
                            ysp = 1792 + sl * 64
                            W("v", sPE, s2mm_t[ghh])
                            emv("scalar_tensor_tensor", rotb[:, ysp:ysp + 64],
                                hidc[:, hh * 64:hh * 64 + 64], dpb[:, hh:hh + 1],
                                (pD if sl == 0 else pF)[:, 0:64], OP.mult, OP.add,
                                inc=(sV.h, 1))
                            yb_free[sl] = sV.tick()
                            emv("tensor_mul", ygc[:, hh * 64:hh * 64 + 64],
                                rotb[:, ysp:ysp + 64], gTc[:, hh * 64:hh * 64 + 64])
                            emv("scalar_tensor_tensor", Rbuf[:, hh * 64:hh * 64 + 64],
                                Rbuf[:, hh * 64:hh * 64 + 64],
                                dtotb[:, c * 128 + hh:c * 128 + hh + 1],
                                (pE_ if sl == 0 else pC)[:, 0:64], OP.mult, OP.add, inc=(sV.h, 1))
                            s2_free[sl] = sV.tick()
                            r2w_t[hh] = s2_free[sl]
                        if 2 <= it_ <= H + 1:
                            hh = it_ - 2
                            ghh = c * H + hh
                            s4 = hh % 4
                            sl = hh % 2
                            W("t", sV, mstt_t[ghh])
                            W("t", sG, prep2g_t[ghh])
                            W("t", sS, prep2s_t[ghh])
                            W("t", sV, yb_free[sl])
                            ybank = pD if sl == 0 else pF
                            emt("matmul", ybank[:, 0:64], rotb[:, 512 + s4 * 128:512 + s4 * 128 + 128],
                                rotb[:, 1536 + s4 * 64:1536 + s4 * 64 + 64], start=True, stop=False)
                            emt("matmul", ybank[:, 0:64], rotb[:, s4 * 128:s4 * 128 + 128],
                                hidc[:, hh * 64:hh * 64 + 64], start=False, stop=True,
                                inc=(sPE.h, 1))
                            ymm_t[ghh] = sPE.tick()
                            W("t", sV, s2_free[sl])
                            sbank = pE_ if sl == 0 else pC
                            if sl == 1:
                                W("t", sV, cb_bank_v[id(pC)])
                            emt("matmul", sbank[:, 0:64], rotb[:, 1024 + s4 * 128:1024 + s4 * 128 + 128],
                                hidc[:, hh * 64:hh * 64 + 64], start=True, stop=True,
                                inc=(sPE.h, 1))
                            s2mm_t[ghh] = sPE.tick()
                    hid_users_pe = sPE.n
                    hid_users_v = sV.n

                    for gg in range(16):
                        emv("bn_stats", bnst3[:, gg * 6:gg * 6 + 6],
                            ygc[:, gg * 512:(gg + 1) * 512])
                    tbn3 = inc_last("v", sV)
                    W("v", sV, tbn3)
                    emv("tensor_mul", gsc[:, 0:32], bnst3[:, 1:96:3], bnst3[:, 1:96:3])
                    emv("scalar_tensor_tensor", gsc[:, 32:64], gsc[:, 0:32], 256.0,
                        bnst3[:, 2:96:3], OP.mult, OP.add)
                    emv("reduce_sum", gns[:, 0:8],
                        gsc[:, 32:64].rearrange("p (g k) -> p g k", k=4), axis=AXL.X)
                    emv("tensor_scalar", gns[:, 8:16], gns[:, 0:8], 1.0 / 1024, EPS, OP.mult, OP.add)
                    tgn2 = inc_last("v", sV)
                    W("s", sV, tgn2)
                    ems("activation", gns[:, 16:24], gns[:, 8:16], AF.Ln)
                    ems("activation", gns[:, 8:16], gns[:, 16:24], AF.Exp, scale=-0.5, inc=(sS.h, 1))
                    tgn3 = sS.tick()
                    W("v", sS, tgn3)
                    for gg in range(8):
                        emv("tensor_scalar_mul", ygc[:, gg * 1024:(gg + 1) * 1024],
                            ygc[:, gg * 1024:(gg + 1) * 1024], gns[:, 8 + gg:9 + gg])
                    tyn = inc_last("v", sV)
                    W("v", stYG.ctr, ygst_dma)
                    W("t", sV, tyn)
                    for it in range(64):
                        def mk_ev7(_it=it):
                            def ev(ps):
                                emv("tensor_scalar_mul", ygst[:, _it * 128:(_it + 1) * 128], ps,
                                    gw_sb[:, _it:_it + 1])
                            return ev
                        pool.add(ygc[:, it * 128:(it + 1) * 128], mk_ev7())
                    tygt = pool.drain()
                    W("y", sV, tygt)
                    ygst_dma = stYG.one(ygt_dd[:, c].rearrange("i p t -> p i t"),
                                        ygst[:].rearrange("p (i t) -> p i t", t=128))

                if dbg_t("ygc7", [128, DIN], BF16):
                    W("y", sV, sV.n)
                    stMISC.one(dbg_outs["ygc7"][:, :], ygc[:])
                mark("P3")
                barrier()

        with ExitStack() as X4:
            E4 = X4.enter_context
            ygT_all = E4(nc.sbuf_tensor("ygT_all", [128, 64 * NCH * 128], BF16))
            wob = E4(nc.sbuf_tensor("wob", [128, 32 * 256], BF16))
            xr4 = E4(nc.sbuf_tensor("xr4", [128, 8 * 256], F32))
            fr4 = E4(nc.sbuf_tensor("fr4", [128, 8 * 256], BF16))
            osum = E4(nc.sbuf_tensor("osum", [128, 8 * 256], F32))

            tyg = stYG.one(ygT_all[:].rearrange("p (i c t) -> p i c t", c=NCH, t=128),
                           ygt_dd.rearrange("i c p t -> p i c t"))
            W("t", stYG.ctr, tyg)

            bank4 = [pA, pA, pB, pB, pC, pC, pD, pD]
            evac_v = 0
            out_batch = {}
            xf_batch = {}
            wo_batches = {}
            for mg in range(16):
                if mg > 0:
                    W("y", sV, evac_v)
                for ch in range(8):
                    stX4.dma(xr4[:, ch * 256:(ch + 1) * 256],
                             x_in[ch * 128:(ch + 1) * 128, mg * 256:(mg + 1) * 256])
                    stX4.dma(fr4[:, ch * 256:(ch + 1) * 256],
                             ffn_dd[ch * 128:(ch + 1) * 128, mg * 256:(mg + 1) * 256])
                xf_batch[mg] = stX4.close()
                for grpw in range(8):
                    wslot0 = (grpw % 4) * 8
                    if (mg, grpw) != (0, 0):
                        pk = wo_batches.get((mg, grpw - 4)) or wo_batches.get((mg - 1, grpw + 4))
                        if pk:
                            W("g", sPE, pk)
                    it0 = grpw * 8
                    stWO.dma(wob[:, wslot0 * 256:(wslot0 + 8) * 256].rearrange(
                                 "p (i k) -> p i k", i=8),
                             wot[it0 * 128:(it0 + 8) * 128,
                                 mg * 256:(mg + 1) * 256].rearrange(
                                 "(i p) k -> p i k", p=128))
                    two = stWO.close()
                    W("t", stWO.ctr, two)
                    if grpw == 0 and mg > 0:
                        W("t", sV, evac_v)
                    for k8 in range(8):
                        it = grpw * 8 + k8
                        for ch in range(8):
                            bank = bank4[ch]
                            half = ch % 2
                            emt("matmul", bank[:, half * 256:(half + 1) * 256],
                                ygT_all[:, (it * 8 + ch) * 128:(it * 8 + ch) * 128 + 128],
                                wob[:, (wslot0 + k8) * 256:(wslot0 + k8 + 1) * 256],
                                start=(it == 0 and half == 0),
                                stop=(it == 63 and half == 1),
                                inc=(sPE.h, 1) if ch == 7 else None)
                        sPE.tick()
                    wo_batches[(mg, grpw)] = sPE.n
                tmm4 = sPE.n
                W("v", sPE, tmm4)
                W("v", stX4.ctr, xf_batch[mg])
                if mg > 0:
                    W("v", stO.ctr, out_batch[mg - 1])
                for ch in range(8):
                    bank = bank4[ch]
                    half = ch % 2
                    sc = ch * 8
                    c1 = stats[:, sc + 3:sc + 4]
                    c2 = stats[:, sc + 4:sc + 5]
                    emv("tensor_scalar", osum[:, ch * 256:(ch + 1) * 256],
                        xr4[:, ch * 256:(ch + 1) * 256], c1, c2, OP.mult, OP.add)
                    emv("tensor_add", osum[:, ch * 256:(ch + 1) * 256],
                        osum[:, ch * 256:(ch + 1) * 256], bank[:, half * 256:(half + 1) * 256])
                    emv("tensor_add", osum[:, ch * 256:(ch + 1) * 256],
                        osum[:, ch * 256:(ch + 1) * 256], fr4[:, ch * 256:(ch + 1) * 256],
                        inc=(sV.h, 1))
                    tev4 = sV.tick()
                    W("y", sV, tev4)
                    stO.dma(out_d[ch * 128:(ch + 1) * 128, mg * 256:(mg + 1) * 256],
                            osum[:, ch * 256:(ch + 1) * 256])
                out_batch[mg] = stO.close()
                evac_v = sV.n

            W("y", stO.ctr, out_batch[15])

        from collections import Counter
        emitted = Counter()
        for q in Q:
            for meth, aa, kk, inc in Q[q]:
                if inc is not None:
                    emitted[inc[0].name if hasattr(inc[0], "name") else id(inc[0])] += inc[1]
        for ctr, nm in [(sPE, "sPE"), (sV, "sV"), (sS, "sS"), (sCC, "sCC"), (sG, "sG")]:
            got = emitted.get(ctr.h.name if hasattr(ctr.h, "name") else id(ctr.h), 0)
            assert got == ctr.n, f"sem {nm}: emitted {got} != tracked {ctr.n}"
        for st_ in YSTREAMS + GSTREAMS:
            got = emitted.get(st_.ctr.h.name if hasattr(st_.ctr.h, "name") else id(st_.ctr.h), 0)
            assert got == st_.ctr.n, f"stream sem: emitted {got} != tracked {st_.ctr.n}"

        if stop_after is not None:
            cut = marks[stop_after]
            for q in Q:
                del Q[q][cut[q]:]
            ys, gs = mark_sems[stop_after]
            for h, v in ys:
                if v > 0:
                    Q["y"].append(("wait_ge", (h, v), {}, None))
            for h, v in gs:
                if v > 0:
                    Q["g"].append(("wait_ge", (h, v), {}, None))

        _insert_pipeline_edges(Q, {"v": (sVp.h, [0]), "s": (sSp.h, [0]), "g": (sGp.h, [0])})
        _coalesce_waits(Q)

        blkB = E(nc.Block())

        def replay(q):
            def f(eng):
                for meth, a, k, inc in Q[q]:
                    r = getattr(eng, meth)(*a, **k)
                    if inc is not None:
                        r.then_inc(inc[0], inc[1])
            return f
        blkB.tensor(replay("t"))
        blkB.vector(replay("v"))
        blkB.scalar(replay("s"))
        blkB.sync(replay("y"))
        blkB.gpsimd(replay("g"))

    if want_queues:
        return nc, dbg_outs, Q
    return nc, dbg_outs


def _prep_inputs(inputs):
    x = np.asarray(inputs["L0_attn_x"], dtype=np.float32)
    w1 = np.ascontiguousarray(np.asarray(inputs["L0_attn_in_proj_w"], np.float32).T)
    wo = np.ascontiguousarray(np.asarray(inputs["L0_attn_out_proj_w"], np.float32).T)
    A = np.asarray(inputs["L0_ffn_A"], np.float32)
    at = np.ascontiguousarray(A.reshape(KFF * RANK, D).T)
    Bm = np.asarray(inputs["L0_ffn_B"], np.float32)
    bm = np.ascontiguousarray(Bm.transpose(0, 2, 1).reshape(KFF * RANK, D))
    wr = np.ascontiguousarray(np.asarray(inputs["L0_ffn_Wr"], np.float32).T)
    com = dict(
        w1t=w1, wot=wo, atw=at, bmw=bm, wrt=wr,
        convw=np.ascontiguousarray(np.asarray(inputs["L0_attn_conv_w"], np.float32)),
        convb=np.ascontiguousarray(np.asarray(inputs["L0_attn_conv_b"], np.float32)),
        dtb=np.ascontiguousarray(np.asarray(inputs["L0_attn_dt_bias"], np.float32)),
        alog=np.ascontiguousarray(np.asarray(inputs["L0_attn_A_log"], np.float32)),
        dprow=np.ascontiguousarray(np.asarray(inputs["L0_attn_Dp"], np.float32)[None, :]),
        normw=np.ascontiguousarray(np.asarray(inputs["L0_attn_norm_w"], np.float32)),
        gnormw=np.ascontiguousarray(np.asarray(inputs["L0_attn_gnorm_w"], np.float32)),
        ident=np.eye(128, dtype=np.float32),
        triu01=np.triu(np.ones((128, 128), np.float32)),
    )
    in_maps = []
    for core in range(NCORES):
        b = core // 4
        r = core % 4
        t0 = r * T
        xs = np.ascontiguousarray(x[b, t0:t0 + T, :])
        halo = np.zeros((4, D), np.float32)
        if r > 0:
            halo[0:3] = x[b, t0 - 3:t0, :]
        sel = np.zeros(4, np.float32)
        sel[:r] = 1.0
        m = dict(com)
        m["x"] = xs
        m["xhalo"] = halo
        m["selmask"] = np.ascontiguousarray(np.tile(sel[None, :], (128, 1)))
        in_maps.append(m)
    return in_maps


_CACHE = {}


def kernel(**inputs):
    if "nc" not in _CACHE:
        _CACHE["nc"] = build_program()[0]
    nc = _CACHE["nc"]
    in_maps = _prep_inputs(inputs)
    res = run_bass_kernel_spmd(nc, in_maps, core_ids=list(range(NCORES)))
    out = np.empty((B, S, D), np.float32)
    for core in range(NCORES):
        b = core // 4
        r = core % 4
        out[b, r * T:(r + 1) * T, :] = res.results[core]["out"]
    return out


# version 15 (final)
# speedup vs baseline: 1.0450x; 1.0450x over previous
import numpy as np
from contextlib import ExitStack

import concourse.bass as bass
import concourse.mybir as mybir
from concourse.bass_utils import run_bass_kernel_spmd
from concourse.bass_types import AP as _APC

F32 = mybir.dt.float32
BF16 = mybir.dt.bfloat16
AF = mybir.ActivationFunctionType
OP = mybir.AluOpType
AXL = mybir.AxisListType

B, S, D = 2, 4096, 4096
H, PD, G, NS = 128, 64, 8, 128
CHUNK = 128
DIN = H * PD
CONVD = DIN + 2 * G * NS
PROJ = DIN + CONVD + H
EPS = 1e-5
NCORES = 8
T = (B * S) // NCORES
NCH = T // CHUNK
KFF, RANK = 64, 16
FFSCALE = 2.0
HBC0 = DIN
DT0 = DIN + CONVD
NEG = -3.0e38
RG = [[0, 1, 2, 3], [4, 5, 6, 7]]


class Ctr:
    def __init__(self, handle):
        self.h = handle
        self.n = 0

    def tick(self, amt=1):
        self.n += amt
        return self.n



def _ap_range(ap):
    try:
        t = ap.tensor
        esz = mybir.dt.size(t.dtype)
        lo = ap.offset
        hi = ap.offset
        for step, count in list(ap.ap)[1:]:
            if count > 1:
                d = step * (count - 1)
                if d > 0:
                    hi += d
                else:
                    lo += d
        return (t.name, lo * esz, (hi + 1) * esz)
    except Exception:
        return None


_RW_METHS = {
    "tensor_copy", "tensor_tensor", "tensor_add", "tensor_sub", "tensor_mul",
    "tensor_max", "tensor_scalar", "tensor_single_scalar", "tensor_scalar_add",
    "tensor_scalar_sub", "tensor_scalar_mul", "tensor_scalar_max", "tensor_scalar_min",
    "tensor_relu", "tensor_reduce", "reduce_max", "reduce_sum",
    "scalar_tensor_tensor", "tensor_tensor_scan", "reciprocal", "bn_stats",
    "bn_aggr", "max", "match_replace", "select", "activation",
    "copy", "add", "mul", "memset", "memzero",
}


def _rw_sets(meth, a, k):
    W_, R_ = [], []
    aps = [x for x in a if hasattr(x, "tensor")]
    kaps = {kk: vv for kk, vv in k.items() if hasattr(vv, "tensor")}
    if meth in _RW_METHS:
        if aps:
            W_.append(aps[0])
            R_.extend(aps[1:])
        if "accum_out" in kaps:
            W_.append(kaps.pop("accum_out"))
        R_.extend(kaps.values())
    return W_, R_


def _overlap(r1, r2):
    return r1 is not None and r2 is not None and r1[0] == r2[0] and r1[1] < r2[2] and r2[1] < r1[2]


def _coalesce_waits(Q):
    for q in Q:
        old = Q[q]
        new = []
        run = {}
        order = []

        def flushrun():
            for h in order:
                new.append(("wait_ge", (h, run[h]), {}, None))
            run.clear()
            order.clear()

        for item in old:
            meth, a, k, inc = item
            if meth == "wait_ge":
                h, v = a
                if h not in run:
                    order.append(h)
                    run[h] = v
                else:
                    run[h] = max(run[h], v)
            else:
                flushrun()
                new.append(item)
        flushrun()
        Q[q] = new


def _insert_pipeline_edges(Q, pipe_sems):
    for q in ("v", "s", "g"):
        psem, pctr = pipe_sems[q]
        old = Q[q]
        new = []
        opsinfo = []
        by_tensor = {}
        sem_counts = {}
        inc_val_of_op = {}
        covered = -1
        for item in old:
            meth, a, k, inc = item
            if meth == "wait_ge":
                new.append(item)
                continue
            W_, R_ = _rw_sets(meth, a, k)
            wr = [r for r in (_ap_range(x) for x in W_) if r]
            rr = [r for r in (_ap_range(x) for x in R_) if r]
            conflict = -1
            for rng, iswrite in [(r, True) for r in wr] + [(r, False) for r in rr]:
                for (lo, hi, opno, w2) in by_tensor.get(rng[0], ()):
                    if opno <= covered or opno <= conflict:
                        continue
                    if not (iswrite or w2):
                        continue
                    if rng[1] < hi and lo < rng[2]:
                        conflict = max(conflict, opno)
            if conflict >= 0:
                nidx = opsinfo[conflict][0]
                m2, a2, k2, inc2 = new[nidx]
                if inc2 is None:
                    pctr[0] += 1
                    new[nidx] = (m2, a2, k2, (psem, 1))
                    inc_val_of_op[conflict] = (psem, pctr[0])
                ws, wv = inc_val_of_op[conflict]
                new.append(("wait_ge", (ws, wv), {}, None))
                covered = conflict
            new.append(item)
            opno = len(opsinfo)
            opsinfo.append((len(new) - 1, wr, rr))
            if inc is not None:
                s_, amt = inc
                cum = sem_counts.get(id(s_), 0) + amt
                sem_counts[id(s_)] = cum
                inc_val_of_op[opno] = (s_, cum)
            for rng in wr:
                by_tensor.setdefault(rng[0], []).append((rng[1], rng[2], opno, True))
            for rng in rr:
                by_tensor.setdefault(rng[0], []).append((rng[1], rng[2], opno, False))
            if opno % 512 == 0:
                for nm in list(by_tensor):
                    by_tensor[nm] = [e for e in by_tensor[nm] if e[2] > covered]
        Q[q] = new


def build_program(dbg=(), stop_after=None, want_queues=False, p1_limit=None):
    dbg = set(dbg)
    nc = bass.Bass("TRN2", target_bir_lowering=False)

    x_in = nc.dram_tensor("x", [T, D], F32, kind="ExternalInput").ap()
    xh_in = nc.dram_tensor("xhalo", [4, D], F32, kind="ExternalInput").ap()
    w1t = nc.dram_tensor("w1t", [D, PROJ], F32, kind="ExternalInput").ap()
    wot = nc.dram_tensor("wot", [DIN, D], F32, kind="ExternalInput").ap()
    at_d = nc.dram_tensor("atw", [D, KFF * RANK], F32, kind="ExternalInput").ap()
    bm_d = nc.dram_tensor("bmw", [KFF * RANK, D], F32, kind="ExternalInput").ap()
    wrt_d = nc.dram_tensor("wrt", [D, KFF], F32, kind="ExternalInput").ap()
    cw_d = nc.dram_tensor("convw", [CONVD, 4], F32, kind="ExternalInput").ap()
    cb_d = nc.dram_tensor("convb", [CONVD], F32, kind="ExternalInput").ap()
    dtb_d = nc.dram_tensor("dtb", [H], F32, kind="ExternalInput").ap()
    alog_d = nc.dram_tensor("alog", [H], F32, kind="ExternalInput").ap()
    dp_d = nc.dram_tensor("dprow", [1, H], F32, kind="ExternalInput").ap()
    nw_d = nc.dram_tensor("normw", [D], F32, kind="ExternalInput").ap()
    gw_d = nc.dram_tensor("gnormw", [DIN], F32, kind="ExternalInput").ap()
    id_d = nc.dram_tensor("ident", [128, 128], F32, kind="ExternalInput").ap()
    tri_d = nc.dram_tensor("triu01", [128, 128], F32, kind="ExternalInput").ap()
    sel_d = nc.dram_tensor("selmask", [128, 4], F32, kind="ExternalInput").ap()

    out_d = nc.dram_tensor("out", [T, D], F32, kind="ExternalOutput").ap()

    hidT_d = nc.dram_tensor("hidT_d", [NCH, CHUNK, DIN], BF16, kind="Internal").ap()
    bc_dd = nc.dram_tensor("bc_d", [16, 128, T], BF16, kind="Internal").ap()
    gate_dd = nc.dram_tensor("gate_d", [64, 128, T], BF16, kind="Internal").ap()
    ffn_dd = nc.dram_tensor("ffn_d", [T, D], BF16, kind="Internal").ap()
    ygt_dd = nc.dram_tensor("ygt_d", [64, NCH, 128, 128], BF16, kind="Internal").ap()
    flat_dd = nc.dram_tensor("flat_d", [NCH, H * CHUNK], F32, kind="Internal").ap()
    ccs_in = nc.dram_tensor("ccs_in", [H, NS * PD], BF16, kind="Internal").ap()
    ccs_out = nc.dram_tensor("ccs_out", [4 * H, NS * PD], BF16, kind="Internal").ap()
    ccl_in = nc.dram_tensor("ccl_in", [H, 1], F32, kind="Internal").ap()
    ccl_out = nc.dram_tensor("ccl_out", [4 * H, 1], F32, kind="Internal").ap()
    sin_dd = nc.dram_tensor("sin_d", [H, NS * PD], F32, kind="Internal").ap()

    dbg_outs = {}

    def dbg_t(name, shape, dt=F32):
        if name in dbg and name not in dbg_outs:
            dbg_outs[name] = nc.dram_tensor("dbg_" + name, shape, dt, kind="ExternalOutput").ap()
        return dbg_outs.get(name)

    Q = {k: [] for k in "tvsyg"}
    marks = {}
    mark_sems = {}

    def mk(q):
        def em(meth, *a, inc=None, **k):
            Q[q].append((meth, a, k, inc))
        return em
    emt, emv, ems, emy, emg = mk("t"), mk("v"), mk("s"), mk("y"), mk("g")
    EM = dict(t=emt, v=emv, s=ems, y=emy, g=emg)

    def inc_last(q, ctr, amt=1):
        meth, aa, kk, inc = Q[q][-1]
        assert inc is None and meth != "wait_ge", (meth, inc)
        t = ctr.tick(amt)
        Q[q][-1] = (meth, aa, kk, (ctr.h, amt))
        return t

    with ExitStack() as CX:
        E = CX.enter_context
        sPE = Ctr(E(nc.semaphore("sPE")))
        sV = Ctr(E(nc.semaphore("sV")))
        sS = Ctr(E(nc.semaphore("sS")))
        sCC = Ctr(E(nc.semaphore("sCC")))

        def W(q, ctr, val=None):
            v = ctr.n if val is None else val
            if v > 0:
                EM[q]("wait_ge", ctr.h, v)

        class DStream:

            def __init__(self, q, name):
                self.q = q
                self.ctr = Ctr(E(nc.semaphore(name)))
                self.open = False

            def dma(self, out, in_):
                if not self.open:
                    W(self.q, self.ctr)
                    self.open = True
                t = self.ctr.tick(16)
                EM[self.q]("dma_start", out, in_, inc=(self.ctr.h, 16))
                return t

            def close(self):
                self.open = False
                return self.ctr.n

            def one(self, out, in_):
                self.dma(out, in_)
                return self.close()

        stMISC = DStream("y", "stMISC")
        stX = DStream("y", "stX")
        stST = DStream("y", "stST")
        stGT = DStream("y", "stGT")
        stCH = DStream("y", "stCH")
        stGF = DStream("y", "stGF")
        stFR = DStream("y", "stFR")
        stSI = DStream("y", "stSI")
        stYG = DStream("y", "stYG")
        stFS = DStream("y", "stFS")
        stO = DStream("y", "stO")
        stX4 = DStream("y", "stX4")
        stW1 = DStream("g", "stW1")
        stW1b = DStream("g", "stW1b")
        stWB = DStream("g", "stWB")
        stWO = DStream("g", "stWO")
        stPK = DStream("g", "stPK")
        YSTREAMS = [stMISC, stX, stST, stGT, stCH, stGF, stFR, stSI, stYG, stFS, stO, stX4]
        GSTREAMS = [stW1, stW1b, stWB, stWO, stPK]

        def mark(name):
            marks[name] = {q: len(Q[q]) for q in Q}
            mark_sems[name] = ([(s.ctr.h, s.ctr.n) for s in YSTREAMS],
                               [(s.ctr.h, s.ctr.n) for s in GSTREAMS])

        def barrier():
            for q in "tvsyg":
                W(q, sPE)
                W(q, sV)
                W(q, sS)
                for st_ in YSTREAMS + GSTREAMS:
                    W(q, st_.ctr)

        pA = E(nc.psum_tensor("pA", [128, 512], F32))
        pB = E(nc.psum_tensor("pB", [128, 512], F32))
        pC = E(nc.psum_tensor("pC", [128, 512], F32))
        pD = E(nc.psum_tensor("pD", [128, 512], F32))
        pE_ = E(nc.psum_tensor("pE", [128, 512], F32))
        pF = E(nc.psum_tensor("pF", [128, 512], F32))
        pG = E(nc.psum_tensor("pG", [128, 1024], BF16))
        pH = E(nc.psum_tensor("pH", [128, 1024], BF16))

        identb = E(nc.sbuf_tensor("identb", [128, 128], BF16))
        identf = E(nc.sbuf_tensor("identf", [128, 128], F32))
        triu = E(nc.sbuf_tensor("triu", [128, 128], F32))
        selm = E(nc.sbuf_tensor("selm", [128, 4], F32))
        ones1 = E(nc.sbuf_tensor("ones1", [1, 128], F32))
        rowbuf = E(nc.sbuf_tensor("rowbuf", [1, 128], F32))
        nw_sb = E(nc.sbuf_tensor("nw_sb", [128, 32], F32))
        gw_sb = E(nc.sbuf_tensor("gw_sb", [128, 64], F32))
        cw_sb = E(nc.sbuf_tensor("cw_sb", [128, 80 * 4], F32))
        cb_sb = E(nc.sbuf_tensor("cb_sb", [128, 80], F32))
        dtb_sb = E(nc.sbuf_tensor("dtb_sb", [128, 1], F32))
        acol = E(nc.sbuf_tensor("acol", [128, 1], F32))
        dpb = E(nc.sbuf_tensor("dpb", [128, 128], F32))
        stats = E(nc.sbuf_tensor("stats", [128, 80], F32))
        smallA = E(nc.sbuf_tensor("smallA", [128, 32], F32))
        aclast = E(nc.sbuf_tensor("aclast", [128, NCH], F32))
        logdt = E(nc.sbuf_tensor("logdt", [128, 1], F32))
        sVp = Ctr(E(nc.semaphore("sVp")))
        sSp = Ctr(E(nc.semaphore("sSp")))
        sGp = Ctr(E(nc.semaphore("sGp")))
        sG = Ctr(E(nc.semaphore("sG")))

        stMISC.dma(identf[:], id_d[:, :])
        stMISC.dma(triu[:], tri_d[:, :])
        stMISC.dma(selm[:], sel_d[:, :])
        stMISC.dma(dtb_sb[:], dtb_d.rearrange("(p one) -> p one", one=1))
        stMISC.dma(smallA[:, 0:1], alog_d.rearrange("(p one) -> p one", one=1))
        stMISC.dma(rowbuf[:], dp_d[:, :])
        for i in range(32):
            stMISC.dma(nw_sb[:, i:i + 1], nw_d[i * 128:(i + 1) * 128].rearrange("(p one) -> p one", one=1))
        for i in range(64):
            stMISC.dma(gw_sb[:, i:i + 1], gw_d[i * 128:(i + 1) * 128].rearrange("(p one) -> p one", one=1))
        stMISC.dma(cw_sb[:].rearrange("p (b k) -> p b k", k=4), cw_d.rearrange("(b p) k -> p b k", p=128))
        for i in range(80):
            stMISC.dma(cb_sb[:, i:i + 1], cb_d[i * 128:(i + 1) * 128].rearrange("(p one) -> p one", one=1))
        pre_io = stMISC.close()

        W("v", stMISC.ctr, pre_io)
        emv("memset", stats[:], 0.0)
        emv("memset", smallA[:, 2:32], 0.0)
        emv("memset", ones1[:], 1.0)
        emv("tensor_copy", identb[:], identf[:])
        W("s", stMISC.ctr, pre_io)
        ems("activation", smallA[:, 1:2], smallA[:, 0:1], AF.Exp, inc=(sS.h, 1))
        tS0 = sS.tick()
        W("v", sS, tS0)
        emv("tensor_scalar_mul", acol[:], smallA[:, 1:2], -1.0)
        tOnes = inc_last("v", sV)
        W("t", sV, tOnes)
        W("t", stMISC.ctr, pre_io)
        emt("matmul", pE_[:, 0:128], ones1[0:1, :], rowbuf[0:1, :], start=True, stop=True,
            inc=(sPE.h, 1))
        tPE0 = sPE.tick()
        W("v", sPE, tPE0)
        emv("tensor_copy", dpb[:], pE_[:, 0:128], inc=(sV.h, 1))
        sV.tick()

        class TPool:
            def __init__(self):
                self.bank = 0
                self.items = []
                self.bank_free = [0, 0]
                self.last_pe = 0

            def add(self, in_ap, evac):
                self.items.append((in_ap, evac))
                if len(self.items) == 8:
                    self.flush()

            def flush(self):
                if not self.items:
                    return
                bankT = pG if self.bank == 0 else pH
                W("t", sV, self.bank_free[self.bank])
                n = len(self.items)
                for i, (in_ap, _) in enumerate(self.items):
                    kk = in_ap.shape[0]
                    emt("transpose", bankT[:, i * 128:i * 128 + kk], in_ap, identb[0:kk, 0:kk],
                        inc=(sPE.h, 1) if i == n - 1 else None)
                self.last_pe = sPE.tick()
                W("v", sPE, self.last_pe)
                for i, (_, evac) in enumerate(self.items):
                    evac(bankT[:, i * 128:i * 128 + 128])
                self.bank_free[self.bank] = inc_last("v", sV)
                self.bank = 1 - self.bank
                self.items = []

            def drain(self):
                self.flush()
                return max(self.bank_free)

        pool = TPool()

        with ExitStack() as AX_:
            EA = AX_.enter_context
            Rbuf = EA(nc.sbuf_tensor("Rbuf", [128, H * PD], F32))
            btok = EA(nc.sbuf_tensor("btok", [128, 64 * 128], BF16))
            dt_sb = EA(nc.sbuf_tensor("dt_sb", [128, T], F32))
            dtT = EA(nc.sbuf_tensor("dtT", [128, T], F32))
            acs = EA(nc.sbuf_tensor("acs", [128, T], F32))
            ddt = EA(nc.sbuf_tensor("ddt", [128, T], F32))
            dtotb = EA(nc.sbuf_tensor("dtotb", [128, T], F32))

            emv("memset", Rbuf[:], 0.0, inc=(sV.h, 1))
            sV.tick()

            with ExitStack() as PX:
                EP = PX.enter_context
                hT = EP(nc.sbuf_tensor("hT", [128, 32 * 1027], BF16))
                w1buf = EP(nc.sbuf_tensor("w1buf", [128, 2 * 4096], BF16))
                scr = EP(nc.sbuf_tensor("scr", [128, 3 * 1024], F32))
                hbcraw = EP(nc.sbuf_tensor("hbcraw", [128, 1027], F32))
                hfeat = EP(nc.sbuf_tensor("hfeat", [128, 1024], BF16))
                hstage = EP(nc.sbuf_tensor("hstage", [128, 2 * 1024], BF16))
                bct = EP(nc.sbuf_tensor("bct", [128, 1024], BF16))
                xh8 = EP(nc.sbuf_tensor("xh8", [128, 8 * 128], BF16))
                bdb = EP(nc.sbuf_tensor("bdb", [128, 2 * 16 * 128], BF16))
                gatef = EP(nc.sbuf_tensor("gatef", [128, 2 * 1024], BF16))
                bnst = EP(nc.sbuf_tensor("bnst", [128, 48], F32))

                def hcol(dt_i, tok):
                    return dt_i * 1027 + tok

                with ExitStack() as X0:
                    xt = X0.enter_context(nc.sbuf_tensor("xt", [128, D], F32))
                    prev_apply = 0
                    for tt in range(9):
                        nrow = 128 if tt < 8 else 4
                        src = x_in[tt * 128:(tt + 1) * 128, :] if tt < 8 else xh_in[0:4, :]
                        W("y", sV, prev_apply)
                        tx = stX.one(xt[0:nrow, :], src)
                        sc = tt * 8
                        m = stats[0:nrow, sc:sc + 1]
                        rs = stats[0:nrow, sc + 1:sc + 2]
                        comb = stats[0:nrow, sc + 2:sc + 3]
                        c1 = stats[0:nrow, sc + 3:sc + 4]
                        c2 = stats[0:nrow, sc + 4:sc + 5]
                        e2 = stats[0:nrow, sc + 5:sc + 6]
                        varp = stats[0:nrow, sc + 6:sc + 7]
                        lncl = stats[0:nrow, sc + 7:sc + 8]
                        W("v", stX.ctr, tx)
                        for hf in range(8):
                            emv("bn_stats", bnst[0:nrow, hf * 6:hf * 6 + 6],
                                xt[0:nrow, hf * 512:hf * 512 + 512])
                        tbn = inc_last("v", sV)
                        W("v", sV, tbn)
                        emv("tensor_mul", smallA[0:nrow, 16:32], bnst[0:nrow, 1:48:3],
                            bnst[0:nrow, 1:48:3])
                        emv("reduce_sum", m, bnst[0:nrow, 1:48:3], axis=AXL.X)
                        emv("tensor_scalar_mul", m, m, 1.0 / 16.0)
                        emv("reduce_sum", e2, bnst[0:nrow, 2:48:3], axis=AXL.X)
                        emv("reduce_sum", varp, smallA[0:nrow, 16:32], axis=AXL.X)
                        emv("scalar_tensor_tensor", e2, varp, 256.0, e2, OP.mult, OP.add)
                        emv("tensor_scalar_mul", e2, e2, 1.0 / D)
                        emv("scalar_tensor_tensor", varp, m, m, e2, OP.mult, OP.subtract)
                        emv("tensor_scalar", varp, varp, -1.0, EPS, OP.mult, OP.add)
                        tv1 = inc_last("v", sV)
                        W("s", sV, tv1)
                        ems("activation", lncl, varp, AF.Ln)
                        ems("activation", rs, lncl, AF.Exp, scale=-0.5, inc=(sS.h, 1))
                        ts1 = sS.tick()
                        W("v", sS, ts1)
                        emv("reciprocal", e2, varp)
                        emv("tensor_scalar", comb, varp, EPS, None, OP.subtract)
                        emv("tensor_mul", comb, comb, e2)
                        emv("tensor_scalar", comb, comb, EPS, None, OP.add)
                        tv2 = inc_last("v", sV)
                        W("s", sV, tv2)
                        ems("activation", lncl, comb, AF.Ln)
                        ems("activation", comb, lncl, AF.Exp, scale=-0.5, inc=(sS.h, 1))
                        ts2 = sS.tick()
                        W("v", sS, ts2)
                        emv("tensor_mul", comb, comb, rs)
                        emv("tensor_scalar", c1, rs, 1.0, None, OP.add)
                        emv("tensor_mul", c2, m, rs)
                        emv("tensor_scalar_mul", c2, c2, -1.0)
                        for dt_i in range(32):
                            slot = dt_i % 8
                            if slot == 0:
                                W("v", sPE, pool.last_pe)
                            xs = xh8[0:nrow, slot * 128:slot * 128 + 128]
                            emv("tensor_scalar", xs, xt[0:nrow, dt_i * 128:dt_i * 128 + 128],
                                m, comb, OP.subtract, OP.mult,
                                inc=(sV.h, 1) if slot == 7 else None)
                            if slot == 7:
                                tap = sV.tick()
                                W("t", sV, tap)

                            def mk_ev(_dt=dt_i, _tt=tt):
                                def ev(ps):
                                    if _tt < 8:
                                        emv("tensor_scalar_mul",
                                            hT[:, hcol(_dt, _tt * 128):hcol(_dt, _tt * 128) + 128],
                                            ps, nw_sb[:, _dt:_dt + 1])
                                    else:
                                        emv("tensor_scalar_mul",
                                            hT[:, hcol(_dt, 1024):hcol(_dt, 1024) + 3],
                                            ps[:, 0:3], nw_sb[:, _dt:_dt + 1])
                                return ev
                            pool.add(xs, mk_ev())
                        pool.flush()
                        prev_apply = sV.n
                    hT_ready = pool.drain()

                if dbg_t("hT", [128, 32 * 1027], BF16):
                    W("y", sV, sV.n)
                    stMISC.one(dbg_outs["hT"][:, :], hT[:])
                if dbg_t("stats", [128, 80]):
                    W("y", sV, sV.n)
                    W("y", sS, sS.n)
                    stMISC.one(dbg_outs["stats"][:, :], stats[:])
                mark("P0")

                blks = [("dt", 0)] + [("bc", j) for j in range(16)] + [("hid", j) for j in range(64)]
                mm_banks = [pA, pB, pC]
                bank_free = {id(pA): 0, id(pB): 0, id(pC): 0}
                halo_free_s = 0
                pE_first = True
                w1_used = {}
                st_bank_free = [0, 0]
                bct_used_pe = 0
                bct_dma = 0
                hst_dma = [0, 0]
                hst_states = [0, 0]
                bd_done = [0, 0]
                gcnt = 0
                pend_states = []

                def emit_states(b_, hpar_, tbd_):
                    g_ = b_ // 8
                    W("t", sV, tbd_)
                    for c in range(NCH):
                        sl = c % 2
                        sbank = pD if sl == 0 else pF
                        W("t", sV, st_bank_free[sl])
                        for hl in (0, 1):
                            emt("matmul", sbank[:, hl * 64:hl * 64 + 64],
                                bdb[:, (hpar_ * 16 + c * 2 + hl) * 128:(hpar_ * 16 + c * 2 + hl) * 128 + 128],
                                hstage[:, hpar_ * 1024 + c * 128 + hl * 64:hpar_ * 1024 + c * 128 + hl * 64 + 64],
                                start=(hl == 0), stop=(hl == 1),
                                inc=(sPE.h, 1) if hl == 1 else None)
                        smm = sPE.tick()
                        W("v", sPE, smm)
                        for hl in (0, 1):
                            hh = 2 * b_ + hl
                            emv("scalar_tensor_tensor", Rbuf[:, hh * 64:hh * 64 + 64],
                                Rbuf[:, hh * 64:hh * 64 + 64],
                                dtotb[:, c * 128 + hh:c * 128 + hh + 1],
                                sbank[:, hl * 64:hl * 64 + 64], OP.mult, OP.add,
                                inc=(sV.h, 1) if hl == 1 else None)
                        st_bank_free[sl] = sV.tick()
                    hst_states[hpar_] = sPE.n

                W("t", sV, hT_ready)

                if p1_limit is not None:
                    blks = blks[:p1_limit]
                for bi, (kind, j) in enumerate(blks):
                    if kind == "dt":
                        c0 = DT0
                    elif kind == "bc":
                        c0 = HBC0 + (64 + j) * 128
                    else:
                        c0 = HBC0 + j * 128
                    par = bi % 2
                    wb = w1buf[:, par * 4096:(par + 1) * 4096]
                    if bi >= 2:
                        W("g", sPE, w1_used[bi - 2])
                    _s = stW1 if par == 0 else stW1b
                    tw = _s.one(wb.rearrange("p (a o) -> p a o", o=128),
                                w1t[:, c0:c0 + 128].rearrange("(a p) o -> p a o", p=128))
                    W("t", _s.ctr, tw)
                    ngrp = 2 if kind == "dt" else 3
                    grp_bank = []
                    for grp in range(ngrp):
                        if grp == 2:
                            bank = pE_
                            W("t", sS, halo_free_s)
                            if pE_first:
                                W("t", sV, sV.n)
                                pE_first = False
                        else:
                            bank = mm_banks[gcnt % 3]
                            gcnt += 1
                            W("t", sS, bank_free[id(bank)])
                        grp_bank.append(bank)
                        for dt_i in range(32):
                            if grp < 2:
                                o = bank[:, 0:512]
                                rhs = hT[:, hcol(dt_i, grp * 512):hcol(dt_i, grp * 512) + 512]
                            else:
                                o = bank[:, 0:3]
                                rhs = hT[:, hcol(dt_i, 1024):hcol(dt_i, 1024) + 3]
                            emt("matmul", o, wb[:, dt_i * 128:dt_i * 128 + 128], rhs,
                                start=(dt_i == 0), stop=(dt_i == 31),
                                inc=(sPE.h, 1) if dt_i == 31 else None)
                        sPE.tick()
                    w1_used[bi] = sPE.n

                    W("s", sPE, sPE.n)
                    if kind == "dt":
                        ems("copy", scr[:, 0:512], grp_bank[0][:, 0:512])
                        ems("copy", scr[:, 512:1024], grp_bank[1][:, 0:512], inc=(sS.h, 1))
                        tev = sS.tick()
                        bank_free[id(grp_bank[0])] = bank_free[id(grp_bank[1])] = tev
                        if dbg_t("dtraw", [128, 1024]):
                            W("y", sS, tev)
                            stMISC.one(dbg_outs["dtraw"][:, :], scr[:, 0:1024])
                        mark("P1mm")
                        dtraw = scr[:, 0:1024]
                        wk1 = scr[:, 1024:2048]
                        wk2 = scr[:, 2048:3072]
                        ems("activation", wk1, dtraw, AF.Abs, bias=dtb_sb[:, 0:1])
                        ems("activation", wk2, wk1, AF.Exp, scale=-1.0)
                        ems("activation", wk1, wk2, AF.Ln, bias=1.0, inc=(sS.h, 1))
                        tch = sS.tick()
                        W("v", sS, tch)
                        emv("tensor_scalar", dt_sb[:], dtraw, dtb_sb[:, 0:1], 0.0, OP.add, OP.max)
                        emv("tensor_add", dt_sb[:], dt_sb[:], wk1)
                        emv("tensor_scalar_mul", wk2, dt_sb[:], acol[:, 0:1])
                        for c in range(NCH):
                            emv("tensor_tensor_scan", acs[:, c * 128:(c + 1) * 128],
                                wk2[:, c * 128:(c + 1) * 128], wk2[:, c * 128:(c + 1) * 128],
                                0.0, OP.add, OP.bypass)
                        emv("tensor_copy", aclast[:], acs[:, 127:T:128])
                        emv("reduce_sum", logdt[:], aclast[:], axis=AXL.X)
                        tdt = inc_last("v", sV)
                        if dbg_t("acs", [128, T]):
                            W("y", sV, tdt)
                            stMISC.one(dbg_outs["acs"][:, :], acs[:])
                        mark("P1sc")
                        W("t", sV, tdt)
                        for c in range(NCH):
                            if c > 0:
                                W("t", sV, sV.n)
                                W("t", sS, sS.n)
                            emt("transpose", pF[:, 0:128], acs[:, c * 128:(c + 1) * 128], identf[:])
                            emt("transpose", pF[:, 128:256], dt_sb[:, c * 128:(c + 1) * 128], identf[:],
                                inc=(sPE.h, 1))
                            t1 = sPE.tick()
                            W("v", sPE, t1)
                            emv("tensor_copy", dtT[:, c * 128:(c + 1) * 128], pF[:, 128:256])
                            emv("tensor_copy", ddt[:, c * 128:(c + 1) * 128], pF[:, 0:128])
                            t2 = inc_last("v", sV)
                            W("t", sV, t2)
                            emt("matmul", pD[0:1, 0:128], aclast[:, c:c + 1], identf[:],
                                start=True, stop=True, inc=(sPE.h, 1))
                            t3 = sPE.tick()
                            W("v", sPE, t3)
                            emv("tensor_copy", rowbuf[0:1, :], pD[0:1, 0:128], inc=(sV.h, 1))
                            t4 = sV.tick()
                            W("t", sV, t4)
                            emt("matmul", pD[:, 0:128], ones1[0:1, :], rowbuf[0:1, :],
                                start=True, stop=True, inc=(sPE.h, 1))
                            t5 = sPE.tick()
                            W("s", sPE, t5)
                            ems("activation", dtotb[:, c * 128:(c + 1) * 128], pD[:, 0:128], AF.Exp,
                                inc=(sS.h, 1))
                            t6 = sS.tick()
                            W("v", sPE, t5)
                            W("v", sS, t6)
                            emv("tensor_sub", wk1[:, 0:128], pD[:, 0:128], ddt[:, c * 128:(c + 1) * 128])
                            t7 = inc_last("v", sV)
                            W("s", sV, t7)
                            ems("activation", wk2[:, 0:128], wk1[:, 0:128], AF.Exp, inc=(sS.h, 1))
                            t8 = sS.tick()
                            W("v", sS, t8)
                            emv("tensor_mul", ddt[:, c * 128:(c + 1) * 128], wk2[:, 0:128],
                                dtT[:, c * 128:(c + 1) * 128], inc=(sV.h, 1))
                            sV.tick()
                        st_bank_free = [sV.n, sV.n]
                        halo_free_s = sS.n
                        if dbg_t("dt", [128, T]):
                            W("y", sV, sV.n)
                            stMISC.one(dbg_outs["dt"][:, :], dt_sb[:])
                        if dbg_t("acs", [128, T]):
                            W("y", sV, sV.n)
                            stMISC.one(dbg_outs["acs"][:, :], acs[:])
                        if dbg_t("ddt", [128, T]):
                            W("y", sV, sV.n)
                            stMISC.one(dbg_outs["ddt"][:, :], ddt[:])
                        if dbg_t("dtotb", [128, T]):
                            W("y", sV, sV.n)
                            stMISC.one(dbg_outs["dtotb"][:, :], dtotb[:])
                        continue

                    ems("copy", hbcraw[:, 3:515], grp_bank[0][:, 0:512])
                    ems("copy", hbcraw[:, 515:1027], grp_bank[1][:, 0:512])
                    ems("copy", hbcraw[:, 0:3], grp_bank[2][:, 0:3], inc=(sS.h, 1))
                    tev = sS.tick()
                    bank_free[id(grp_bank[0])] = bank_free[id(grp_bank[1])] = tev
                    halo_free_s = tev
                    cblk = (64 + j) if kind == "bc" else j
                    W("v", sS, tev)
                    emv("tensor_scalar", scr[:, 0:1024], hbcraw[:, 3:1027],
                        cw_sb[:, cblk * 4 + 3:cblk * 4 + 4], cb_sb[:, cblk:cblk + 1], OP.mult, OP.add)
                    for i in (2, 1, 0):
                        emv("scalar_tensor_tensor", scr[:, 0:1024], hbcraw[:, i:i + 1024],
                            cw_sb[:, cblk * 4 + i:cblk * 4 + i + 1], scr[:, 0:1024], OP.mult, OP.add)
                    tconv = inc_last("v", sV)
                    W("s", sV, tconv)

                    if kind == "bc":
                        W("s", stST.ctr, bct_dma)
                        W("s", sPE, bct_used_pe)
                        ems("activation", bct[:, 0:1024], scr[:, 0:1024], AF.Silu, inc=(sS.h, 1))
                        tsil = sS.tick()
                        W("y", sS, tsil)
                        bct_dma = stST.one(bc_dd[j, :, :], bct[:, 0:1024])
                        if j < 8:
                            W("t", sS, tsil)
                            for c in range(NCH):
                                def mk_ev2(_c=c, _g=j):
                                    def ev(ps):
                                        emv("tensor_copy",
                                            btok[:, (_c * 8 + _g) * 128:(_c * 8 + _g) * 128 + 128], ps)
                                    return ev
                                pool.add(bct[:, c * 128:(c + 1) * 128], mk_ev2())
                            pool.flush()
                            bct_used_pe = pool.last_pe
                        continue

                    b = j
                    g = b // 8
                    hpar = b % 2
                    ems("activation", hfeat[:, 0:1024], scr[:, 0:1024], AF.Silu, inc=(sS.h, 1))
                    tsil = sS.tick()
                    if pend_states:
                        emit_states(*pend_states.pop(0))
                    W("v", stST.ctr, hst_dma[hpar])
                    W("v", sPE, hst_states[hpar])
                    W("t", sS, tsil)
                    for c in range(NCH):
                        def mk_ev3(_c=c, _hp=hpar):
                            def ev(ps):
                                emv("tensor_copy",
                                    hstage[:, _hp * 1024 + _c * 128:_hp * 1024 + (_c + 1) * 128], ps)
                            return ev
                        pool.add(hfeat[:, c * 128:(c + 1) * 128], mk_ev3())
                    pool.flush()
                    thid = sV.n
                    W("y", sV, thid)
                    hst_dma[hpar] = stST.one(
                        hidT_d[:, :, b * 128:(b + 1) * 128].rearrange("c t o -> t c o"),
                        hstage[:, hpar * 1024:hpar * 1024 + 1024].rearrange("t (c o) -> t c o", o=128))
                    W("v", sPE, hst_states[hpar])
                    for c in range(NCH):
                        for hl in (0, 1):
                            hh = 2 * b + hl
                            emv("tensor_scalar_mul",
                                bdb[:, (hpar * 16 + c * 2 + hl) * 128:(hpar * 16 + c * 2 + hl) * 128 + 128],
                                btok[:, (c * 8 + g) * 128:(c * 8 + g) * 128 + 128],
                                ddt[:, c * 128 + hh:c * 128 + hh + 1])
                    tbd = inc_last("v", sV)
                    pend_states.append((b, hpar, tbd))

                while pend_states:
                    emit_states(*pend_states.pop(0))
                pool.drain()
                p1_v = sV.n

                if dbg_t("Rloc", [128, H * PD]):
                    W("y", sV, p1_v)
                    stMISC.one(dbg_outs["Rloc"][:, :], Rbuf[:])
                if dbg_t("btok", [128, 64 * 128], BF16):
                    W("y", sV, p1_v)
                    stMISC.one(dbg_outs["btok"][:, :], btok[:])
                mark("P1")

                W("g", sV, p1_v)
                for hh in range(H):
                    stPK.dma(ccs_in[hh, :].rearrange("(n p) -> n p", n=NS),
                             Rbuf[:, hh * 64:hh * 64 + 64])
                pack_t = stPK.close()
                W("y", sV, p1_v)
                ccl_t = stSI.one(ccl_in[:, 0:1], logdt[:])
                W("g", stPK.ctr, pack_t)
                W("g", stSI.ctr, ccl_t)
                emg("collective_compute", "AllGather", OP.bypass, replica_groups=RG,
                    ins=[ccs_in[:, :].opt()], outs=[ccs_out[:, :].opt()], inc=(sCC.h, 1))
                sCC.tick()
                emg("collective_compute", "AllGather", OP.bypass, replica_groups=RG,
                    ins=[ccl_in[:, :].opt()], outs=[ccl_out[:, :].opt()], inc=(sCC.h, 1))
                cc_done = sCC.tick()
                mark("CC")

                gate_dmas = {}
                for b in range(64):
                    c0 = b * 128
                    bi = len(blks) + b
                    par = bi % 2
                    wb = w1buf[:, par * 4096:(par + 1) * 4096]
                    W("g", sPE, w1_used.get(bi - 2, 0))
                    _s = stW1 if par == 0 else stW1b
                    tw = _s.one(wb.rearrange("p (a o) -> p a o", o=128),
                                w1t[:, c0:c0 + 128].rearrange("(a p) o -> p a o", p=128))
                    W("t", _s.ctr, tw)
                    gpar = b % 2
                    for grp in range(2):
                        bank = mm_banks[gcnt % 3]
                        gcnt += 1
                        W("t", sS, bank_free[id(bank)])
                        for dt_i in range(32):
                            emt("matmul", bank[:, 0:512], wb[:, dt_i * 128:dt_i * 128 + 128],
                                hT[:, hcol(dt_i, grp * 512):hcol(dt_i, grp * 512) + 512],
                                start=(dt_i == 0), stop=(dt_i == 31),
                                inc=(sPE.h, 1) if dt_i == 31 else None)
                        sPE.tick()
                        w1_used[bi] = sPE.n
                        W("s", sPE, sPE.n)
                        if grp == 0 and b >= 2:
                            W("s", stGT.ctr, gate_dmas[b - 2])
                        ems("activation", gatef[:, gpar * 1024 + grp * 512:gpar * 1024 + grp * 512 + 512],
                            bank[:, 0:512], AF.Silu, inc=(sS.h, 1))
                        bank_free[id(bank)] = sS.tick()
                    W("y", sS, sS.n)
                    gate_dmas[b] = stGT.one(gate_dd[b, :, :], gatef[:, gpar * 1024:gpar * 1024 + 1024])
                mark("P2")
                barrier()

            with ExitStack() as FX:
                EF = FX.enter_context
                at_sb = EF(nc.sbuf_tensor("at_sb", [128, 32 * 1024], BF16))
                wrt_sb = EF(nc.sbuf_tensor("wrt_sb", [128, 32 * 64], BF16))
                xre = EF(nc.sbuf_tensor("xre", [128, 2 * 2048], F32))
                xnt = EF(nc.sbuf_tensor("xnt", [128, 2 * 4096], BF16))
                xnb = EF(nc.sbuf_tensor("xnb", [128, 8 * 128], BF16))
                hgb = EF(nc.sbuf_tensor("hgb", [128, 1024], BF16))
                hgT = EF(nc.sbuf_tensor("hgT", [128, 64 * 128], BF16))
                scoreb = EF(nc.sbuf_tensor("scoreb", [128, 160], F32))
                bmb = EF(nc.sbuf_tensor("bmb", [128, 8 * 512], BF16))
                ffst = EF(nc.sbuf_tensor("ffst", [128, 4 * 512], BF16))

                tat = stWB.one(at_sb[:].rearrange("p (a k) -> p a k", k=1024),
                               at_d.rearrange("(a p) k -> p a k", p=128))
                twr = stWB.one(wrt_sb[:].rearrange("p (a k) -> p a k", k=64),
                               wrt_d.rearrange("(a p) k -> p a k", p=128))

                hffn_free = 0
                score_free = 0
                xnt_users = [0, 0]
                xre_users = [0, 0]
                txn_t = {}
                xld = {0: stX.one(xre[:, 0:2048], x_in[0:128, 0:2048])}
                for tt in range(9):
                    if tt >= 1:
                        pt = tt - 1
                        xsl = pt % 2
                        base = xsl * 4096
                        sbof = 0
                        W("t", sV, txn_t[pt])
                        W("t", stWB.ctr, twr)
                        if score_free:
                            W("t", sV, score_free)
                        for dt_i in range(32):
                            emt("matmul", pE_[:, 0:64], xnt[:, base + dt_i * 128:base + (dt_i + 1) * 128],
                                wrt_sb[:, dt_i * 64:(dt_i + 1) * 64], start=(dt_i == 0), stop=(dt_i == 31),
                                inc=(sPE.h, 1) if dt_i == 31 else None)
                        tscm = sPE.tick()
                        if hffn_free:
                            W("t", sV, hffn_free)
                        hmm = 0
                        for krg in range(2):
                            bank = pA if krg == 0 else pB
                            for dt_i in range(32):
                                emt("matmul", bank[:, 0:512], xnt[:, base + dt_i * 128:base + (dt_i + 1) * 128],
                                    at_sb[:, dt_i * 1024 + krg * 512:dt_i * 1024 + krg * 512 + 512],
                                    start=(dt_i == 0), stop=(dt_i == 31),
                                    inc=(sPE.h, 1) if dt_i == 31 else None)
                            hmm = sPE.tick()
                        xnt_users[xsl] = sPE.n
                        W("v", sPE, tscm)
                        emv("tensor_copy", scoreb[:, sbof + 0:sbof + 64], pE_[:, 0:64])
                        emv("max", scoreb[:, sbof + 64:sbof + 72], scoreb[:, sbof + 0:sbof + 64])
                        emv("match_replace", scoreb[:, sbof + 80:sbof + 144],
                            scoreb[:, sbof + 64:sbof + 72], scoreb[:, sbof + 0:sbof + 64], NEG)
                        emv("max", scoreb[:, sbof + 72:sbof + 80], scoreb[:, sbof + 80:sbof + 144])
                        emv("tensor_scalar_mul", scoreb[:, sbof + 144:sbof + 145],
                            scoreb[:, sbof + 64:sbof + 65], -1.0)
                        score_free = inc_last("v", sV)
                        W("s", sV, score_free)
                        ems("activation", scoreb[:, sbof + 80:sbof + 144], scoreb[:, sbof + 0:sbof + 64],
                            AF.Exp, bias=scoreb[:, sbof + 144:sbof + 145], inc=(sS.h, 1))
                        tg2 = sS.tick()
                        W("v", sS, tg2)
                        emv("tensor_scalar", scoreb[:, sbof + 0:sbof + 64], scoreb[:, sbof + 0:sbof + 64],
                            scoreb[:, sbof + 79:sbof + 80], None, OP.is_ge)
                        emv("tensor_mul", scoreb[:, sbof + 80:sbof + 144],
                            scoreb[:, sbof + 80:sbof + 144], scoreb[:, sbof + 0:sbof + 64])
                        emv("reduce_sum", scoreb[:, sbof + 145:sbof + 146],
                            scoreb[:, sbof + 80:sbof + 144], axis=AXL.X)
                        emv("reciprocal", scoreb[:, sbof + 146:sbof + 147],
                            scoreb[:, sbof + 145:sbof + 146])
                        emv("tensor_scalar", scoreb[:, sbof + 80:sbof + 144],
                            scoreb[:, sbof + 80:sbof + 144], scoreb[:, sbof + 146:sbof + 147], FFSCALE,
                            OP.mult, OP.mult)
                        W("v", sPE, hmm)
                        W("v", sPE, pool.last_pe)
                        for k in range(64):
                            bank = pA if k < 32 else pB
                            emv("tensor_scalar_mul", hgb[:, k * 16:(k + 1) * 16],
                                bank[:, (k % 32) * 16:(k % 32) * 16 + 16],
                                scoreb[:, sbof + 80 + k:sbof + 81 + k])
                        hffn_free = inc_last("v", sV)
                        W("t", sV, hffn_free)
                        for krt in range(8):
                            def mk_ev5(_tt=pt, _krt=krt):
                                def ev(ps):
                                    emv("tensor_copy",
                                        hgT[:, (_tt * 8 + _krt) * 128:(_tt * 8 + _krt) * 128 + 128], ps)
                                return ev
                            pool.add(hgb[:, krt * 128:(krt + 1) * 128], mk_ev5())
                        pool.flush()
                    if tt < 8:
                        sc = tt * 8
                        m = stats[:, sc:sc + 1]
                        rs = stats[:, sc + 1:sc + 2]
                        xsl_t = tt % 2
                        for half in range(2):
                            k = tt * 2 + half
                            nk = k + 1
                            if nk < 16:
                                ntt, nhalf = nk // 2, nk % 2
                                W("y", sV, xre_users[nk % 2])
                                xld[nk] = stX.one(xre[:, (nk % 2) * 2048:(nk % 2 + 1) * 2048],
                                                  x_in[ntt * 128:(ntt + 1) * 128,
                                                       nhalf * 2048:(nhalf + 1) * 2048])
                            W("v", stX.ctr, xld[k])
                            if half == 0:
                                W("v", sPE, xnt_users[xsl_t])
                            tap = 0
                            for qq in range(16):
                                dt_i = half * 16 + qq
                                slot = dt_i % 8
                                if slot == 0:
                                    W("v", sPE, pool.last_pe)
                                xs = xnb[:, slot * 128:slot * 128 + 128]
                                emv("tensor_scalar", xs,
                                    xre[:, (k % 2) * 2048 + qq * 128:(k % 2) * 2048 + qq * 128 + 128],
                                    m, rs, OP.subtract, OP.mult,
                                    inc=(sV.h, 1) if slot == 7 else None)
                                if slot == 7:
                                    tap = sV.tick()
                                    W("t", sV, tap)

                                def mk_ev4(_dt=dt_i, _b=xsl_t * 4096):
                                    def ev(ps):
                                        emv("tensor_copy", xnt[:, _b + _dt * 128:_b + (_dt + 1) * 128], ps)
                                    return ev
                                pool.add(xs, mk_ev4())
                            pool.flush()
                            xre_users[k % 2] = tap
                        txn_t[tt] = pool.drain()

                hgT_ready = pool.drain()

                if dbg_t("g2", [128, 64]):
                    W("y", sV, sV.n)
                    stMISC.one(dbg_outs["g2"][:, :], scoreb[:, 80:144])
                if dbg_t("hgT", [128, 64 * 128], BF16):
                    W("y", sV, sV.n)
                    stMISC.one(dbg_outs["hgT"][:, :], hgT[:])

                ob_free = {id(pC): 0, id(pD): 0}
                bm_used = 0
                ffs_v = {}
                for mg in range(8):
                    W("g", sPE, bm_used)
                    for krt in range(8):
                        stWB.dma(bmb[:, krt * 512:(krt + 1) * 512],
                                 bm_d[krt * 128:(krt + 1) * 128, mg * 512:(mg + 1) * 512])
                    tbm = stWB.close()
                    for tokt in range(8):
                        bank = pC if tokt % 2 == 0 else pD
                        W("t", sV, ob_free[id(bank)])
                        if tokt == 0:
                            W("t", sV, hgT_ready)
                            W("t", stWB.ctr, tbm)
                        for krt in range(8):
                            emt("matmul", bank[:, 0:512],
                                hgT[:, (tokt * 8 + krt) * 128:(tokt * 8 + krt) * 128 + 128],
                                bmb[:, krt * 512:(krt + 1) * 512],
                                start=(krt == 0), stop=(krt == 7),
                                inc=(sPE.h, 1) if krt == 7 else None)
                        tmm = sPE.tick()
                        bm_used = tmm
                        W("v", sPE, tmm)
                        fslot = tokt % 4
                        if mg > 0 and tokt == 0:
                            W("v", stFS.ctr, ffs_b)
                        if tokt == 4:
                            W("v", stFS.ctr, ffs_a)
                        emv("tensor_copy", ffst[:, fslot * 512:(fslot + 1) * 512], bank[:, 0:512],
                            inc=(sV.h, 1))
                        ob_free[id(bank)] = sV.tick()
                        ffs_v[tokt] = ob_free[id(bank)]
                        W("y", sV, ffs_v[tokt])
                        stFS.dma(ffn_dd[tokt * 128:(tokt + 1) * 128, mg * 512:(mg + 1) * 512],
                                 ffst[:, fslot * 512:(fslot + 1) * 512])
                        if tokt == 3:
                            ffs_a = stFS.close()
                    ffs_b = stFS.close()
                mark("P25")
                barrier()

            with ExitStack() as X3:
                E3 = X3.enter_context
                acsT = E3(nc.sbuf_tensor("acsT", [128, T], F32))
                cbt = E3(nc.sbuf_tensor("cbt", [128, 8 * 128], F32))
                e4 = E3(nc.sbuf_tensor("e4", [128, 2 * 512], BF16))
                hidc = E3(nc.sbuf_tensor("hidc", [128, DIN], BF16))
                gTc = E3(nc.sbuf_tensor("gTc", [128, DIN], BF16))
                ygc = E3(nc.sbuf_tensor("ygc", [128, DIN], BF16))
                bcc = E3(nc.sbuf_tensor("bcc", [128, 16 * 128], BF16))
                ygst = E3(nc.sbuf_tensor("ygst", [128, DIN], BF16))
                gf8 = E3(nc.sbuf_tensor("gf8", [128, 8 * 128], BF16))
                rbb = E3(nc.sbuf_tensor("rbb", [128, 2 * 512], F32))
                rot = E3(nc.sbuf_tensor("rot", [128, 4 * 128], F32))
                texb = E3(nc.sbuf_tensor("texb", [128, 4 * 128], BF16))
                rotb = E3(nc.sbuf_tensor("rotb", [128, 1920], BF16))
                sinw = E3(nc.sbuf_tensor("sinw", [128, 32], F32))
                gns = E3(nc.sbuf_tensor("gns", [128, 24], F32))
                bnst3 = E3(nc.sbuf_tensor("bnst3", [128, 96], F32))
                gsc = E3(nc.sbuf_tensor("gsc", [128, 64], F32))

                for c in range(NCH):
                    if c > 0:
                        W("t", sV, sV.n)
                    emt("transpose", pF[:, 0:128], acs[:, c * 128:(c + 1) * 128], identf[:],
                        inc=(sPE.h, 1))
                    tt1 = sPE.tick()
                    W("v", sPE, tt1)
                    emv("tensor_copy", acsT[:, c * 128:(c + 1) * 128], pF[:, 0:128], inc=(sV.h, 1))
                    sV.tick()

                W("y", sCC, cc_done)
                for jj in range(4):
                    stSI.dma(sinw[:, jj:jj + 1], ccl_out[jj * 128:(jj + 1) * 128, 0:1])
                lw_t = stSI.close()
                W("v", stSI.ctr, lw_t)
                emv("tensor_mul", sinw[:, 4:8], sinw[:, 0:4], selm[:, 0:4])
                emv("tensor_tensor_scan", sinw[:, 8:12], sinw[:, 4:8], sinw[:, 4:8], 0.0,
                    OP.add, OP.bypass)
                emv("reduce_sum", sinw[:, 12:13], sinw[:, 4:8], axis=AXL.X)
                emv("tensor_scalar", sinw[:, 16:20], sinw[:, 8:12], sinw[:, 12:13], -1.0,
                    OP.subtract, OP.mult)
                tw1 = inc_last("v", sV)
                W("s", sV, tw1)
                ems("activation", sinw[:, 20:24], sinw[:, 16:20], AF.Exp, inc=(sS.h, 1))
                tw2 = sS.tick()
                W("v", sS, tw2)
                emv("tensor_mul", sinw[:, 20:24], sinw[:, 20:24], selm[:, 0:4])
                emv("memset", Rbuf[:], 0.0, inc=(sV.h, 1))
                sV.tick()
                for jj in range(4):
                    W("y", sV, sV.n)
                    tld = stSI.one(hidc[:, 0:DIN], ccs_out[jj * 128:(jj + 1) * 128, :])
                    W("v", stSI.ctr, tld)
                    emv("scalar_tensor_tensor", Rbuf[:], hidc[:, 0:DIN], sinw[:, 20 + jj:21 + jj],
                        Rbuf[:], OP.mult, OP.add, inc=(sV.h, 1))
                    sV.tick()
                W("g", sV, sV.n)
                tso = stPK.one(sin_dd[:, :], Rbuf[:])
                W("y", stPK.ctr, tso)
                for hh in range(H):
                    stSI.dma(Rbuf[:, hh * 64:hh * 64 + 64], sin_dd[hh, :].rearrange("(n p) -> n p", n=NS))
                tsin = stSI.close()
                W("v", stSI.ctr, tsin)
                emv("nop", inc=(sV.h, 1))
                r2_ready = sV.tick()
                if dbg_t("Sin", [128, H * PD]):
                    W("y", sV, r2_ready)
                    stMISC.one(dbg_outs["Sin"][:, :], Rbuf[:])

                hid_users_pe = 0
                hid_users_v = r2_ready
                ygst_dma = 0
                yb_free = [0, 0]
                s2_free = [0, 0]
                rb_free = [(0, 0), (0, 0)]
                cb_bank_v = {id(pA): 0, id(pB): 0, id(pC): 0}
                frow_t = [0, 0]
                rb_t = {}
                texp_t = {}
                mstt_t = {}
                prep2_t = {}
                prep2g_t = {}
                prep2s_t = {}
                r2w_t = {hh_: r2_ready for hh_ in range(H)}
                ymm_t = {}
                s2mm_t = {}
                e4_user = [0, 0]

                hid_users_g = 0
                for c in range(NCH):
                    W("y", sPE, hid_users_pe)
                    W("y", sV, hid_users_v)
                    W("y", sG, hid_users_g)
                    stCH.dma(hidc[:, 0:DIN], hidT_d[c, :, :])
                    stCH.dma(bcc[:, 0:16 * 128].rearrange("p (b t) -> p b t", b=16),
                             bc_dd[:, :, c * 128:(c + 1) * 128].rearrange("b p t -> p b t"))
                    stCH.dma(flat_dd[c, :].rearrange("(hh i) -> hh i", hh=128),
                             acs[:, c * 128:(c + 1) * 128])
                    tch = stCH.close()
                    for ob8 in range(8):
                        W("y", sPE, pool.last_pe)
                        stGF.dma(gf8[:, 0:1024].rearrange("p (o t) -> p o t", o=8),
                                 gate_dd[ob8 * 8:(ob8 + 1) * 8, :,
                                         c * 128:(c + 1) * 128].rearrange("o p t -> p o t"))
                        tg = stGF.close()
                        W("t", stGF.ctr, tg)
                        for slot in range(8):
                            ob = ob8 * 8 + slot

                            def mk_ev6(_ob=ob):
                                def ev(ps):
                                    emv("tensor_copy", gTc[:, _ob * 128:(_ob + 1) * 128], ps)
                                return ev
                            pool.add(gf8[:, slot * 128:slot * 128 + 128], mk_ev6())
                    pool.drain()
                    W("t", stCH.ctr, tch)
                    for gg in range(8):
                        bank = mm_banks[gg % 3]
                        W("t", sV, cb_bank_v[id(bank)])
                        if gg == 2:
                            W("t", sV, s2_free[1])
                        emt("matmul", bank[:, 0:128], bcc[:, gg * 128:(gg + 1) * 128],
                            bcc[:, (8 + gg) * 128:(8 + gg) * 128 + 128],
                            start=True, stop=True, inc=(sPE.h, 1))
                        tcb = sPE.tick()
                        W("v", sPE, tcb)
                        emv("tensor_mul", cbt[:, gg * 128:(gg + 1) * 128], bank[:, 0:128], triu[:],
                            inc=(sV.h, 1))
                        cb_bank_v[id(bank)] = sV.tick()

                    W("v", stCH.ctr, tch)

                    def stage_rb(hh):
                        hg4 = hh // 4
                        fsl = hg4 % 2
                        if hh % 4 == 0:
                            W("y", sV, rb_free[fsl][0])
                            W("y", sS, rb_free[fsl][1])
                            W("y", stCH.ctr, tch)
                            sl_ = flat_dd[c, hg4 * 512:(hg4 + 1) * 512]
                            bap = _APC(tensor=sl_.tensor, offset=sl_.offset,
                                       ap=[[0, 128], [1, 512]])
                            stFR.dma(rbb[:, fsl * 512:(fsl + 1) * 512], bap)
                            rb_t[(c * 32) + hg4] = stFR.close()
                            W("s", stFR.ctr, rb_t[(c * 32) + hg4])
                            W("s", sG, e4_user[fsl])
                            ems("activation", e4[:, fsl * 512:(fsl + 1) * 512],
                                rbb[:, fsl * 512:(fsl + 1) * 512], AF.Exp, inc=(sS.h, 1))
                            rb_free[fsl] = (rb_free[fsl][0], sS.tick())

                    for it_ in range(H + 3):
                        if it_ < H:
                            hh = it_
                            ghh = c * H + hh
                            stage_rb(hh)
                            hg4 = hh // 4
                            fsl = hg4 % 2
                            s4 = hh % 4
                            W("v", stFR.ctr, rb_t[(c * 32) + hg4])
                            W("v", sS, texp_t.get(ghh - 4, 0))
                            emv("tensor_scalar", rot[:, s4 * 128:s4 * 128 + 128],
                                rbb[:, fsl * 512 + (hh % 4) * 128:fsl * 512 + (hh % 4) * 128 + 128],
                                acsT[:, c * 128 + hh:c * 128 + hh + 1], 0.0,
                                OP.subtract, OP.min, inc=(sV.h, 1))
                            t0t = sV.tick()
                            rb_free[fsl] = (t0t, rb_free[fsl][1])
                            W("s", sV, t0t)
                            W("s", sV, mstt_t.get(ghh - 4, 0))
                            ems("activation", texb[:, s4 * 128:s4 * 128 + 128],
                                rot[:, s4 * 128:s4 * 128 + 128], AF.Exp, inc=(sS.h, 1))
                            texp_t[ghh] = sS.tick()
                        if 1 <= it_ <= H:
                            hh = it_ - 1
                            ghh = c * H + hh
                            gg = hh // 16
                            hg4 = hh // 4
                            fsl = hg4 % 2
                            s4 = hh % 4
                            W("v", sS, texp_t[ghh])
                            W("v", sPE, ymm_t.get(ghh - 4, 0))
                            W("v", sPE, s2mm_t.get(ghh - 4, 0))
                            emv("scalar_tensor_tensor", rotb[:, s4 * 128:s4 * 128 + 128],
                                texb[:, s4 * 128:s4 * 128 + 128],
                                dtT[:, c * 128 + hh:c * 128 + hh + 1],
                                cbt[:, gg * 128:(gg + 1) * 128], OP.mult, OP.mult, inc=(sV.h, 1))
                            mstt_t[ghh] = sV.tick()
                            W("g", sPE, ymm_t.get(ghh - 4, 0))
                            W("g", sPE, s2mm_t.get(ghh - 4, 0))
                            W("g", sS, rb_free[fsl][1])
                            emg("tensor_mul", rotb[:, 512 + s4 * 128:512 + s4 * 128 + 128],
                                bcc[:, (8 + gg) * 128:(8 + gg) * 128 + 128],
                                e4[:, fsl * 512 + (hh % 4) * 128:fsl * 512 + (hh % 4) * 128 + 128],
                                inc=(sG.h, 1))
                            e4_user[fsl] = sG.tick()
                            W("g", sV, r2w_t.get(hh, 0))
                            emg("tensor_copy", rotb[:, 1536 + s4 * 64:1536 + s4 * 64 + 64],
                                Rbuf[:, hh * 64:hh * 64 + 64], inc=(sG.h, 1))
                            prep2g_t[ghh] = sG.tick()
                            hid_users_g = prep2g_t[ghh]
                            W("s", sPE, ymm_t.get(ghh - 4, 0))
                            W("s", sPE, s2mm_t.get(ghh - 4, 0))
                            ems("activation", rotb[:, 1024 + s4 * 128:1024 + s4 * 128 + 128],
                                btok[:, (c * 8 + gg) * 128:(c * 8 + gg) * 128 + 128],
                                AF.Copy, scale=ddt[:, c * 128 + hh:c * 128 + hh + 1],
                                inc=(sS.h, 1))
                            prep2s_t[ghh] = sS.tick()
                        if it_ >= 3 and it_ - 3 < H:
                            hh = it_ - 3
                            ghh = c * H + hh
                            sl = hh % 2
                            ysp = 1792 + sl * 64
                            W("v", sPE, s2mm_t[ghh])
                            emv("scalar_tensor_tensor", rotb[:, ysp:ysp + 64],
                                hidc[:, hh * 64:hh * 64 + 64], dpb[:, hh:hh + 1],
                                (pD if sl == 0 else pF)[:, 0:64], OP.mult, OP.add,
                                inc=(sV.h, 1))
                            yb_free[sl] = sV.tick()
                            emv("tensor_mul", ygc[:, hh * 64:hh * 64 + 64],
                                rotb[:, ysp:ysp + 64], gTc[:, hh * 64:hh * 64 + 64])
                            emv("scalar_tensor_tensor", Rbuf[:, hh * 64:hh * 64 + 64],
                                Rbuf[:, hh * 64:hh * 64 + 64],
                                dtotb[:, c * 128 + hh:c * 128 + hh + 1],
                                (pE_ if sl == 0 else pC)[:, 0:64], OP.mult, OP.add, inc=(sV.h, 1))
                            s2_free[sl] = sV.tick()
                            r2w_t[hh] = s2_free[sl]
                        if 2 <= it_ <= H + 1:
                            hh = it_ - 2
                            ghh = c * H + hh
                            s4 = hh % 4
                            sl = hh % 2
                            W("t", sV, mstt_t[ghh])
                            W("t", sG, prep2g_t[ghh])
                            W("t", sS, prep2s_t[ghh])
                            W("t", sV, yb_free[sl])
                            ybank = pD if sl == 0 else pF
                            emt("matmul", ybank[:, 0:64], rotb[:, 512 + s4 * 128:512 + s4 * 128 + 128],
                                rotb[:, 1536 + s4 * 64:1536 + s4 * 64 + 64], start=True, stop=False)
                            emt("matmul", ybank[:, 0:64], rotb[:, s4 * 128:s4 * 128 + 128],
                                hidc[:, hh * 64:hh * 64 + 64], start=False, stop=True,
                                inc=(sPE.h, 1))
                            ymm_t[ghh] = sPE.tick()
                            W("t", sV, s2_free[sl])
                            sbank = pE_ if sl == 0 else pC
                            if sl == 1:
                                W("t", sV, cb_bank_v[id(pC)])
                            emt("matmul", sbank[:, 0:64], rotb[:, 1024 + s4 * 128:1024 + s4 * 128 + 128],
                                hidc[:, hh * 64:hh * 64 + 64], start=True, stop=True,
                                inc=(sPE.h, 1))
                            s2mm_t[ghh] = sPE.tick()
                    hid_users_pe = sPE.n
                    hid_users_v = sV.n

                    for gg in range(16):
                        emv("bn_stats", bnst3[:, gg * 6:gg * 6 + 6],
                            ygc[:, gg * 512:(gg + 1) * 512])
                    tbn3 = inc_last("v", sV)
                    W("v", sV, tbn3)
                    emv("tensor_mul", gsc[:, 0:32], bnst3[:, 1:96:3], bnst3[:, 1:96:3])
                    emv("scalar_tensor_tensor", gsc[:, 32:64], gsc[:, 0:32], 256.0,
                        bnst3[:, 2:96:3], OP.mult, OP.add)
                    emv("reduce_sum", gns[:, 0:8],
                        gsc[:, 32:64].rearrange("p (g k) -> p g k", k=4), axis=AXL.X)
                    emv("tensor_scalar", gns[:, 8:16], gns[:, 0:8], 1.0 / 1024, EPS, OP.mult, OP.add)
                    tgn2 = inc_last("v", sV)
                    W("s", sV, tgn2)
                    ems("activation", gns[:, 16:24], gns[:, 8:16], AF.Ln)
                    ems("activation", gns[:, 8:16], gns[:, 16:24], AF.Exp, scale=-0.5, inc=(sS.h, 1))
                    tgn3 = sS.tick()
                    W("v", sS, tgn3)
                    for gg in range(8):
                        emv("tensor_scalar_mul", ygc[:, gg * 1024:(gg + 1) * 1024],
                            ygc[:, gg * 1024:(gg + 1) * 1024], gns[:, 8 + gg:9 + gg])
                    tyn = inc_last("v", sV)
                    W("v", stYG.ctr, ygst_dma)
                    W("t", sV, tyn)
                    for it in range(64):
                        def mk_ev7(_it=it):
                            def ev(ps):
                                emv("tensor_scalar_mul", ygst[:, _it * 128:(_it + 1) * 128], ps,
                                    gw_sb[:, _it:_it + 1])
                            return ev
                        pool.add(ygc[:, it * 128:(it + 1) * 128], mk_ev7())
                    tygt = pool.drain()
                    W("y", sV, tygt)
                    ygst_dma = stYG.one(ygt_dd[:, c].rearrange("i p t -> p i t"),
                                        ygst[:].rearrange("p (i t) -> p i t", t=128))

                if dbg_t("ygc7", [128, DIN], BF16):
                    W("y", sV, sV.n)
                    stMISC.one(dbg_outs["ygc7"][:, :], ygc[:])
                mark("P3")
                barrier()

        with ExitStack() as X4:
            E4 = X4.enter_context
            ygT_all = E4(nc.sbuf_tensor("ygT_all", [128, 64 * NCH * 128], BF16))
            wob = E4(nc.sbuf_tensor("wob", [128, 32 * 256], BF16))
            xr4 = E4(nc.sbuf_tensor("xr4", [128, 8 * 256], F32))
            fr4 = E4(nc.sbuf_tensor("fr4", [128, 8 * 256], BF16))
            osum = E4(nc.sbuf_tensor("osum", [128, 8 * 256], F32))

            tyg = stYG.one(ygT_all[:].rearrange("p (i c t) -> p i c t", c=NCH, t=128),
                           ygt_dd.rearrange("i c p t -> p i c t"))
            W("t", stYG.ctr, tyg)

            bank4 = [pA, pA, pB, pB, pC, pC, pD, pD]
            evac_v = 0
            out_batch = {}
            xf_batch = {}
            wo_batches = {}
            for mg in range(16):
                if mg > 0:
                    W("y", sV, evac_v)
                for ch in range(8):
                    stX4.dma(xr4[:, ch * 256:(ch + 1) * 256],
                             x_in[ch * 128:(ch + 1) * 128, mg * 256:(mg + 1) * 256])
                    stX4.dma(fr4[:, ch * 256:(ch + 1) * 256],
                             ffn_dd[ch * 128:(ch + 1) * 128, mg * 256:(mg + 1) * 256])
                xf_batch[mg] = stX4.close()
                for grpw in range(8):
                    wslot0 = (grpw % 4) * 8
                    if (mg, grpw) != (0, 0):
                        pk = wo_batches.get((mg, grpw - 4)) or wo_batches.get((mg - 1, grpw + 4))
                        if pk:
                            W("g", sPE, pk)
                    it0 = grpw * 8
                    stWO.dma(wob[:, wslot0 * 256:(wslot0 + 8) * 256].rearrange(
                                 "p (i k) -> p i k", i=8),
                             wot[it0 * 128:(it0 + 8) * 128,
                                 mg * 256:(mg + 1) * 256].rearrange(
                                 "(i p) k -> p i k", p=128))
                    two = stWO.close()
                    W("t", stWO.ctr, two)
                    if grpw == 0 and mg > 0:
                        W("t", sV, evac_v)
                    for k8 in range(8):
                        it = grpw * 8 + k8
                        for ch in range(8):
                            bank = bank4[ch]
                            half = ch % 2
                            emt("matmul", bank[:, half * 256:(half + 1) * 256],
                                ygT_all[:, (it * 8 + ch) * 128:(it * 8 + ch) * 128 + 128],
                                wob[:, (wslot0 + k8) * 256:(wslot0 + k8 + 1) * 256],
                                start=(it == 0 and half == 0),
                                stop=(it == 63 and half == 1),
                                inc=(sPE.h, 1) if ch == 7 else None)
                        sPE.tick()
                    wo_batches[(mg, grpw)] = sPE.n
                tmm4 = sPE.n
                W("v", sPE, tmm4)
                W("v", stX4.ctr, xf_batch[mg])
                if mg > 0:
                    W("v", stO.ctr, out_batch[mg - 1])
                for ch in range(8):
                    bank = bank4[ch]
                    half = ch % 2
                    sc = ch * 8
                    c1 = stats[:, sc + 3:sc + 4]
                    c2 = stats[:, sc + 4:sc + 5]
                    emv("tensor_scalar", osum[:, ch * 256:(ch + 1) * 256],
                        xr4[:, ch * 256:(ch + 1) * 256], c1, c2, OP.mult, OP.add)
                    emv("tensor_add", osum[:, ch * 256:(ch + 1) * 256],
                        osum[:, ch * 256:(ch + 1) * 256], bank[:, half * 256:(half + 1) * 256])
                    emv("tensor_add", osum[:, ch * 256:(ch + 1) * 256],
                        osum[:, ch * 256:(ch + 1) * 256], fr4[:, ch * 256:(ch + 1) * 256],
                        inc=(sV.h, 1))
                    tev4 = sV.tick()
                    W("y", sV, tev4)
                    stO.dma(out_d[ch * 128:(ch + 1) * 128, mg * 256:(mg + 1) * 256],
                            osum[:, ch * 256:(ch + 1) * 256])
                out_batch[mg] = stO.close()
                evac_v = sV.n

            W("y", stO.ctr, out_batch[15])

        from collections import Counter
        emitted = Counter()
        for q in Q:
            for meth, aa, kk, inc in Q[q]:
                if inc is not None:
                    emitted[inc[0].name if hasattr(inc[0], "name") else id(inc[0])] += inc[1]
        for ctr, nm in [(sPE, "sPE"), (sV, "sV"), (sS, "sS"), (sCC, "sCC"), (sG, "sG")]:
            got = emitted.get(ctr.h.name if hasattr(ctr.h, "name") else id(ctr.h), 0)
            assert got == ctr.n, f"sem {nm}: emitted {got} != tracked {ctr.n}"
        for st_ in YSTREAMS + GSTREAMS:
            got = emitted.get(st_.ctr.h.name if hasattr(st_.ctr.h, "name") else id(st_.ctr.h), 0)
            assert got == st_.ctr.n, f"stream sem: emitted {got} != tracked {st_.ctr.n}"

        if stop_after is not None:
            cut = marks[stop_after]
            for q in Q:
                del Q[q][cut[q]:]
            ys, gs = mark_sems[stop_after]
            for h, v in ys:
                if v > 0:
                    Q["y"].append(("wait_ge", (h, v), {}, None))
            for h, v in gs:
                if v > 0:
                    Q["g"].append(("wait_ge", (h, v), {}, None))

        _insert_pipeline_edges(Q, {"v": (sVp.h, [0]), "s": (sSp.h, [0]), "g": (sGp.h, [0])})
        _coalesce_waits(Q)

        blkB = E(nc.Block())

        def replay(q):
            def f(eng):
                for meth, a, k, inc in Q[q]:
                    r = getattr(eng, meth)(*a, **k)
                    if inc is not None:
                        r.then_inc(inc[0], inc[1])
            return f
        blkB.tensor(replay("t"))
        blkB.vector(replay("v"))
        blkB.scalar(replay("s"))
        blkB.sync(replay("y"))
        blkB.gpsimd(replay("g"))

    if want_queues:
        return nc, dbg_outs, Q
    return nc, dbg_outs


def _prep_inputs(inputs):
    x = np.asarray(inputs["L0_attn_x"], dtype=np.float32)
    w1 = np.ascontiguousarray(np.asarray(inputs["L0_attn_in_proj_w"], np.float32).T)
    wo = np.ascontiguousarray(np.asarray(inputs["L0_attn_out_proj_w"], np.float32).T)
    A = np.asarray(inputs["L0_ffn_A"], np.float32)
    at = np.ascontiguousarray(A.reshape(KFF * RANK, D).T)
    Bm = np.asarray(inputs["L0_ffn_B"], np.float32)
    bm = np.ascontiguousarray(Bm.transpose(0, 2, 1).reshape(KFF * RANK, D))
    wr = np.ascontiguousarray(np.asarray(inputs["L0_ffn_Wr"], np.float32).T)
    com = dict(
        w1t=w1, wot=wo, atw=at, bmw=bm, wrt=wr,
        convw=np.ascontiguousarray(np.asarray(inputs["L0_attn_conv_w"], np.float32)),
        convb=np.ascontiguousarray(np.asarray(inputs["L0_attn_conv_b"], np.float32)),
        dtb=np.ascontiguousarray(np.asarray(inputs["L0_attn_dt_bias"], np.float32)),
        alog=np.ascontiguousarray(np.asarray(inputs["L0_attn_A_log"], np.float32)),
        dprow=np.ascontiguousarray(np.asarray(inputs["L0_attn_Dp"], np.float32)[None, :]),
        normw=np.ascontiguousarray(np.asarray(inputs["L0_attn_norm_w"], np.float32)),
        gnormw=np.ascontiguousarray(np.asarray(inputs["L0_attn_gnorm_w"], np.float32)),
        ident=np.eye(128, dtype=np.float32),
        triu01=np.triu(np.ones((128, 128), np.float32)),
    )
    in_maps = []
    for core in range(NCORES):
        b = core // 4
        r = core % 4
        t0 = r * T
        xs = np.ascontiguousarray(x[b, t0:t0 + T, :])
        halo = np.zeros((4, D), np.float32)
        if r > 0:
            halo[0:3] = x[b, t0 - 3:t0, :]
        sel = np.zeros(4, np.float32)
        sel[:r] = 1.0
        m = dict(com)
        m["x"] = xs
        m["xhalo"] = halo
        m["selmask"] = np.ascontiguousarray(np.tile(sel[None, :], (128, 1)))
        in_maps.append(m)
    return in_maps


_CACHE = {}


def kernel(**inputs):
    if "nc" not in _CACHE:
        _CACHE["nc"] = build_program()[0]
    nc = _CACHE["nc"]
    in_maps = _prep_inputs(inputs)
    res = run_bass_kernel_spmd(nc, in_maps, core_ids=list(range(NCORES)))
    out = np.empty((B, S, D), np.float32)
    for core in range(NCORES):
        b = core // 4
        r = core % 4
        out[b, r * T:(r + 1) * T, :] = res.results[core]["out"]
    return out
